# version 22
# baseline (speedup 1.0000x reference)
import contextlib
import numpy as np

import concourse.bass as bass
import concourse.mybir as mybir
from concourse.bass_utils import run_bass_kernel_spmd

F32 = mybir.dt.float32
BF16 = mybir.dt.bfloat16
AF = mybir.ActivationFunctionType
ALU = mybir.AluOpType
AX = mybir.AxisListType

P = 128
E = 1024
CE = 8
DF = 4096
CF = 32
H = 16
TT = 512
NT = 8
TPC = TT * NT
NSUB = 4
NCORE = 8

EPS_LN = 1e-5
EPS_ATT = 1e-6
EPS_RMS = 1e-8


def build_program():
    nc = bass.Bass("TRN2", target_bir_lowering=False)

    xT = nc.dram_tensor("xT", [E, TPC], F32, kind="ExternalInput").ap()
    wqT = nc.dram_tensor("wqT", [E, E], F32, kind="ExternalInput").ap()
    wkTe = nc.dram_tensor("wkTe", [E + 1, E], F32, kind="ExternalInput").ap()
    wvTe = nc.dram_tensor("wvTe", [E + 1, E], F32, kind="ExternalInput").ap()
    woT = nc.dram_tensor("woT", [E, E], F32, kind="ExternalInput").ap()
    bq_d = nc.dram_tensor("bq", [E], F32, kind="ExternalInput").ap()
    bo_d = nc.dram_tensor("bo", [E], F32, kind="ExternalInput").ap()
    wupT = nc.dram_tensor("wupT", [E, DF], F32, kind="ExternalInput").ap()
    selE_d = nc.dram_tensor("selE", [H, CE * P], F32, kind="ExternalInput").ap()
    wdnT = nc.dram_tensor("wdnT", [DF, E], F32, kind="ExternalInput").ap()
    outT = nc.dram_tensor("outT", [E, TPC], F32, kind="ExternalOutput").ap()

    phiq_d = nc.dram_tensor("phiq_d", [E, TPC], BF16, kind="Internal").ap()
    wupt_d = nc.dram_tensor("wupt_d", [E, DF], BF16, kind="Internal").ap()
    wdnt_d = nc.dram_tensor("wdnt_d", [CE, P, CF, P], BF16, kind="Internal").ap()
    cc_in = nc.dram_tensor("cc_in", [P, 1032], F32, kind="Internal").ap()
    cc_out = nc.dram_tensor("cc_out", [P, 1032], F32, kind="Internal").ap()

    A = nc.alloc_sbuf_tensor
    xa = A("xa", [P, 2, CE, TT], F32)
    xn = A("xn", [P, CE, TT], BF16)
    bq_sb = A("bq_sb", [P, CE], F32)
    bo_sb = A("bo_sb", [P, CE], F32)
    ones128 = A("ones128", [P, P], BF16)
    ones1 = A("ones1", [1, P], BF16)
    ones1f = A("ones1f", [1, P], F32)
    onescol = A("onescol", [P, 1], F32)
    cst = A("cst", [P, 8], F32)
    st = [A(f"st{i}", [P, TT], F32) for i in range(4)]
    xbfr = A("xbfr", [P, 2, TT], BF16)
    xsqr = A("xsqr", [P, 2, TT], BF16)
    hsqr = A("hsqr", [P, 4, TT], BF16)
    phiK = A("phiK", [P, NSUB, E], BF16)
    phiq_sb = A("phiq_sb", [P, CE, TT], BF16)
    Sbf = A("Sbf", [P, 8 * 129], BF16)
    Zm = A("Zm", [P, CE, H], BF16)
    Ec = A("Ec", [H, CE, P], BF16)
    rec16b = A("rec16b", [H, TT], BF16)
    den16 = A("den16", [H, TT], F32)
    alp = A("alp", [P, 2], F32)
    alpp = A("alpp", [P, 16], F32)
    alpd = A("alpd", [P, CF], F32)
    alptree = A("alptree", [P, H], F32)
    row128 = A("row128", [1, P], F32)
    s2t = A("s2t", [1, 1], F32)
    throw = A("throw", [1, 2], F32)
    thrU = A("thrU", [P, 2], F32)
    thrD = A("thrD", [P, 2], F32)
    t4 = A("t4", [P, TT], F32)
    b2_sb = A("b2_sb", [P, TT], F32)
    selE_f = A("selE_f", [H, CE * P], F32)

    TOP = 229344
    AT = nc.alloc_sbuf_tensor_at
    REG_A = TOP - 32768
    wk_sb = AT("wk_sb", [P, CE, E], BF16, offset=REG_A)
    wv_sb = AT("wv_sb", [P, CE, E], BF16, offset=REG_A + 16384)
    upbuf = AT("upbuf", [P, 2, CE, TT], BF16, offset=REG_A)
    dnbuf = AT("dnbuf", [P, 2, CF, P], BF16, offset=REG_A + 16384)
    wostag = AT("wostag", [P, 4, E], F32, offset=REG_A + 16384)

    REG_B = REG_A - 32768
    stag = AT("stag", [P, CE, E], F32, offset=REG_B)
    h_sb = AT("h_sb", [P, CF, TT], BF16, offset=REG_B)
    ternb = AT("ternb", [P, DF], F32, offset=REG_B)
    ternbh = AT("ternbh", [P, 3, E], F32, offset=REG_B)
    terno = AT("terno", [P, 3, E], BF16, offset=REG_B + 12288)
    tb2 = AT("tb2", [P, 3, E], BF16, offset=REG_B + 18432)
    adbuf = AT("adbuf", [P, 2, E], F32, offset=REG_B + 24576)

    REG_C = REG_B - 16384
    wq_sb = AT("wq_sb", [P, CE, E], BF16, offset=REG_C)
    wo_sb = AT("wo_sb", [P, CE, E], BF16, offset=REG_C)

    REG_D = REG_C - 16448
    vext = AT("vext", [P, NSUB, 8 * 129], BF16, offset=REG_D)
    pa = AT("pa", [P, 2, TT], F32, offset=REG_D + 8256)
    pc = AT("pc", [P, 2, TT], F32, offset=REG_D + 12352)
    r_sb = AT("r_sb", [P, CE, TT], F32, offset=REG_D)

    REG_E = REG_D - 4096
    wkrow = AT("wkrow", [1, E], BF16, offset=REG_E)
    wvrow = AT("wvrow", [1, E], BF16, offset=REG_E + 2048)
    dbs0 = AT("dbs0", [P, TT], F32, offset=REG_E)
    dbs1 = AT("dbs1", [P, TT], F32, offset=REG_E + 2048)

    REG_H = REG_E - 4128
    S_sb = AT("S_sb", [P, 8 * 129], F32, offset=REG_H)
    out_sb = AT("out_sb", [P, 2, TT], F32, offset=REG_H)

    def r_ap(c):
        return r_sb[:, c, :]

    def dbs_ap(c):
        return (dbs0 if c % 2 == 0 else dbs1)[:, :]


    PS = [nc.alloc_psum_tensor(f"ps{i}", [P, TT], F32) for i in range(8)]

    def s_slice(c):
        return PS[5 + c // 3][:, (c % 3) * 129:(c % 3) * 129 + 129]

    cnt = {}
    waited = {}
    E_ = {}
    ENGINES = ("sync", "scalar", "vector", "gpsimd", "tensor")

    class _MockInstr:
        def then_inc(self, *a, **k):
            return self

    class _MockQ:
        def __getattr__(self, name):
            def f(*a, **k):
                return _MockInstr()
            return f

    class _Sec:
        def __init__(self):
            self.lists = {e: [] for e in ENGINES}
            self.mock = _MockQ()
            self.real = False

        def sec(self, name):
            def deco(fn):
                fn(self.mock)
                self.lists[name].append(fn)
            return deco

    SB = _Sec()

    with contextlib.ExitStack() as stack:
        sem = {}
        for s in ("T", "V", "S",
                  "DX0", "DX1", "DC", "DO0", "DO1",
                  "AW", "AU0", "AU1", "AD0", "AD1",
                  "GQ", "GA", "GB0", "GB1", "GT0", "GT1", "GT2",
                  "GO0", "GO1", "GO2", "GS", "GF", "PP", "AE0", "AE1"):
            sem[s] = stack.enter_context(nc.semaphore(s))
            cnt[s] = 0

        def inc(instr, s, n=1):
            instr.then_inc(sem[s], n)
            cnt[s] += n
            return (s, cnt[s])

        def ev(s):
            return (s, cnt[s])

        def mkwt(ename):
            def wt(q, *evs):
                for e in evs:
                    if e is None:
                        continue
                    (s, v) = e
                    if v <= 0:
                        continue
                    key = (ename, s)
                    if waited.get(key, 0) < v:
                        q.wait_ge(sem[s], v)
                        waited[key] = v
            return wt

        WT = {n: mkwt(n) for n in ENGINES}

        class _BlockShim:
            def __getattr__(self, name):
                return SB.sec(name)

        block = _BlockShim()

        pa_lastS = [None, None]
        pc_lastV = [None, None]

        def emit_phi(slot, ps_ap, bias, out_ap, mm_ev, extra_wV=None, extra_out_wV=None):
            prev_paS = pa_lastS[slot]
            prev_pcV = pc_lastV[slot]

            @block.vector
            def _(q):
                wt = WT["vector"]
                wt(q, mm_ev)
                if prev_paS is not None:
                    wt(q, prev_paS)
                if extra_wV:
                    wt(q, *extra_wV)
                if bias is None:
                    i = inc(q.tensor_scalar(out=pa[:, slot, :], in0=ps_ap, scalar1=0.0,
                                            scalar2=None, op0=ALU.min), "V")
                else:
                    i = inc(q.tensor_scalar(out=pa[:, slot, :], in0=ps_ap, scalar1=bias,
                                            scalar2=0.0, op0=ALU.add, op1=ALU.min), "V")
                E_["_phimin"] = i

            minev = E_["_phimin"]

            @block.scalar
            def _(q):
                wt = WT["scalar"]
                wt(q, minev)
                q.activation(out=pa[:, slot, :], in_=pa[:, slot, :], func=AF.Exp)
                if prev_pcV is not None:
                    wt(q, prev_pcV)
                if bias is None:
                    i = inc(q.activation(out=pc[:, slot, :], in_=ps_ap, func=AF.Relu), "S")
                else:
                    i = inc(q.activation(out=pc[:, slot, :], in_=ps_ap, func=AF.Relu,
                                         bias=bias), "S")
                E_["_phirelu"] = i

            reluev = E_["_phirelu"]
            pa_lastS[slot] = reluev

            @block.vector
            def _(q):
                wt = WT["vector"]
                wt(q, reluev)
                if extra_out_wV:
                    wt(q, *extra_out_wV)
                i = inc(q.tensor_add(out=out_ap, in0=pa[:, slot, :], in1=pc[:, slot, :]), "V")
                E_["_phitt"] = i

            ttev = E_["_phitt"]
            pc_lastV[slot] = ttev
            return minev, reluev, ttev

        @block.sync
        def _(q):
            for c in range(CE):
                i = inc(q.dma_start(out=xa[:, 0, c, :], in_=xT[c * P:(c + 1) * P, 0:TT]),
                        "DX0", 16)
            E_["x", 0] = i
            i = inc(q.dma_start(out=bq_sb[:, :], in_=bq_d.rearrange("(c p) -> p c", p=P),
                                allow_slow_non_contiguous=True), "DC", 16)
            i = inc(q.dma_start(out=bo_sb[:, :], in_=bo_d.rearrange("(c p) -> p c", p=P),
                                allow_slow_non_contiguous=True), "DC", 16)
            E_["bias"] = i
            i = inc(q.dma_start(out=selE_f[0:H, :], in_=selE_d), "DC", 16)
            E_["selE_ld"] = i

        @block.vector
        def _(q):
            q.memset(ones128[:, :], 1.0)
            q.memset(ones1[:, :], 1.0)
            q.memset(ones1f[:, :], 1.0)
            q.memset(onescol[:, :], 1.0)
            q.memset(cst[:, 0:1], EPS_LN)
            q.memset(cst[:, 1:2], -0.5)
            q.memset(cst[:, 2:3], 0.5)
            q.memset(cst[:, 3:4], 4.0 / DF)
            q.memset(cst[:, 4:5], 0.0)
            vv = vext.ap().rearrange("p j (c k) -> p j c k", k=129)
            q.memset(vv[:, :, :, 128:129], 1.0)
            q.memset(Zm[:, :, :], 0.0)
            wt = WT["vector"]
            wt(q, E_["selE_ld"])
            i = q.tensor_copy(out=Ec[0:H, :, :].rearrange("h c p -> h (c p)"),
                              in_=selE_f[0:H, :])
            E_["consts"] = inc(i, "V")

        wspec = [("wk", wkTe, wk_sb, wkrow), ("wv", wvTe, wv_sb, wvrow),
                 ("wq", wqT, wq_sb, None)]
        for name, wsrc, dst, rowdst in wspec:
            @block.scalar
            def _(q, name=name, wsrc=wsrc):
                wt = WT["scalar"]
                if name == "wv":
                    wt(q, E_["cast_wv_row"])
                if name == "wq":
                    wt(q, E_["cast_wq_pre"])
                i = inc(q.dma_start(out=stag[:, :, :],
                                    in_=wsrc[0:E, :].rearrange("(c p) f -> p c f", p=P)),
                        "AW", 16)
                E_["wload_" + name] = i

            @block.vector
            def _(q, name=name, dst=dst):
                wt = WT["vector"]
                wt(q, E_["wload_" + name])
                i = inc(q.tensor_copy(out=dst[:, :, :], in_=stag[:, :, :]), "V")
                E_["castm_" + name] = i

            if rowdst is not None:
                @block.scalar
                def _(q, name=name, wsrc=wsrc):
                    wt = WT["scalar"]
                    wt(q, E_["castm_" + name])
                    i = inc(q.dma_start(out=stag[0:1, 0, 0:E], in_=wsrc[E:E + 1, :]),
                            "AW", 16)
                    E_["wrowload_" + name] = i

                @block.vector
                def _(q, name=name, rowdst=rowdst):
                    wt = WT["vector"]
                    wt(q, E_["wrowload_" + name])
                    i = inc(q.tensor_copy(out=rowdst[0:1, :], in_=stag[0:1, 0, 0:E]), "V")
                    if name == "wk":
                        E_["cast_wv_row"] = i
                        E_["cast_wk"] = i
                    else:
                        E_["cast_wq_pre"] = i
                        E_["cast_wv"] = i
            else:
                E_["cast_wq"] = E_["castm_wq"]

        tern_idx = [0]

        def emit_tern_chunk(kind, r):
            gi = tern_idx[0]
            tern_idx[0] += 1
            sl = gi % 3

            @block.sync
            def _(q, kind=kind, r=r, gi=gi, sl=sl):
                wt = WT["sync"]
                if gi in (0, 1, 2):
                    wt(q, E_["alup_red", 15])
                if gi >= 3:
                    wt(q, E_["t_sg", gi - 3])
                if kind == "tu":
                    rc, hh = r // 4, r % 4
                    i = inc(q.dma_start(out=ternbh[:, sl, :],
                                        in_=wupT[rc * P:(rc + 1) * P, hh * E:(hh + 1) * E]),
                            f"GT{sl}", 16)
                else:
                    i = inc(q.dma_start(out=ternbh[:, sl, :],
                                        in_=wdnT[r * P:(r + 1) * P, 0:E]), f"GT{sl}", 16)
                E_["t_ld", gi] = i

            @block.scalar
            def _(q, kind=kind, gi=gi, sl=sl):
                wt = WT["scalar"]
                thr_t = thrU if kind == "tu" else thrD
                wt(q, E_["t_ld", gi], E_["thr_done", 0 if kind == "tu" else 1])
                if gi >= 3:
                    wt(q, E_["t_out", gi - 3])
                q.activation(out=terno[:, sl, :], in_=ternbh[:, sl, :],
                             func=AF.Sign, bias=thr_t[:, 1:2])
                i = inc(q.activation(out=tb2[:, sl, :], in_=ternbh[:, sl, :],
                                     func=AF.Sign, bias=thr_t[:, 0:1]), "S")
                E_["t_sg", gi] = i

            @block.gpsimd
            def _(q, kind=kind, r=r, gi=gi, sl=sl):
                wt = WT["gpsimd"]
                wt(q, E_["t_sg", gi])
                q.tensor_add(out=terno[:, sl, :], in0=terno[:, sl, :],
                             in1=tb2[:, sl, :])
                if kind == "tu":
                    rc, hh = r // 4, r % 4
                    i = inc(q.dma_start(out=wupt_d[rc * P:(rc + 1) * P, hh * E:(hh + 1) * E],
                                        in_=terno[:, sl, :]), f"GO{sl}", 16)
                else:
                    for dc in range(CE):
                        i = inc(q.dma_start(out=wdnt_d[dc, :, r, :],
                                            in_=terno[:, sl, dc * P:(dc + 1) * P]), f"GO{sl}", 16)
                E_["t_out", gi] = i
                if kind == "tu" and r == CF - 1:
                    E_["tu_done"] = i
                if kind == "td" and r == CF - 1:
                    E_["td_done"] = i

        TERN_SCHED = {4: [("tu", r) for r in range(0, 16)],
                      5: [("tu", r) for r in range(16, 32)],
                      6: [("td", r) for r in range(0, 16)],
                      7: [("td", r) for r in range(16, 32)]}

        def emit_alup(r):
            sl = r % 2

            @block.gpsimd
            def _(q, r=r, sl=sl):
                wt = WT["gpsimd"]
                if r < 2:
                    wt(q, E_["cast_wq"])
                else:
                    wt(q, E_["alup_red", r - 2])
                rc, hh = r // 2, r % 2
                i = inc(q.dma_start(out=ternb[:, sl * 2048:(sl + 1) * 2048],
                                    in_=wupT[rc * P:(rc + 1) * P, hh * 2048:(hh + 1) * 2048]),
                        f"GT{sl}", 16)
                E_["alup_ld", r] = i

            @block.vector
            def _(q, r=r, sl=sl):
                wt = WT["vector"]
                wt(q, E_["alup_ld", r])
                i = q.tensor_reduce(out=alpp[:, r:r + 1],
                                    in_=ternb[:, sl * 2048:(sl + 1) * 2048], axis=AX.X,
                                    op=ALU.add, apply_absolute_value=True)
                E_["alup_red", r] = inc(i, "V")

        def emit_aldn(r):
            @block.gpsimd
            def _(q, r=r):
                wt = WT["gpsimd"]
                if r >= 2:
                    wt(q, E_["aldn_red", r - 2])
                i = inc(q.dma_start(out=adbuf[:, r % 2, :],
                                    in_=wdnT[r * P:(r + 1) * P, 0:E]), f"GB{r % 2}", 16)
                E_["aldn_ld", r] = i

            @block.vector
            def _(q, r=r):
                wt = WT["vector"]
                wt(q, E_["aldn_ld", r])
                i = q.tensor_reduce(out=alpd[:, r:r + 1], in_=adbuf[:, r % 2, :],
                                    axis=AX.X, op=ALU.add,
                                    apply_absolute_value=True)
                E_["aldn_red", r] = inc(i, "V")

        BG_SCHED = {0: [(lambda r=r: emit_alup(r)) for r in range(8)],
                    1: [(lambda r=r: emit_alup(r)) for r in range(8, 16)],
                    2: [(lambda r=r: emit_aldn(r)) for r in range(16)],
                    3: [(lambda r=r: emit_aldn(r)) for r in range(16, 32)]}

        def emit_tern_some(t, n):
            lst = BG_SCHED.get(t)
            if lst:
                k = 1 if t < 2 else 2
                for _ in range(min(k if n == 2 else len(lst), len(lst))):
                    lst.pop(0)()
            lst = TERN_SCHED.get(t)
            if not lst:
                return
            for _ in range(min(n, len(lst))):
                kind, r = lst.pop(0)
                emit_tern_chunk(kind, r)

        p1_slot_mm = {0: None, 1: None}

        def emit_ln1(tt):
            buf = tt % 2
            for c in range(CE):
                prev_slot = p1_slot_mm[c % 2]

                @block.vector
                def _(q, tt=tt, c=c, buf=buf, prev_slot=prev_slot):
                    wt = WT["vector"]
                    wt(q, E_["x", tt])
                    if prev_slot is not None:
                        wt(q, prev_slot)
                    q.tensor_copy(out=xbfr[:, c % 2, :], in_=xa[:, buf, c, :])
                    i = inc(q.tensor_mul(out=xsqr[:, c % 2, :], in0=xbfr[:, c % 2, :],
                                         in1=xbfr[:, c % 2, :]), "V")
                    E_["p1sx", tt, c] = i

                @block.tensor
                def _(q, tt=tt, c=c):
                    wt = WT["tensor"]
                    if c == 0:
                        wt(q, E_["consts"])
                        if tt > 0:
                            wt(q, E_["p1facr", tt - 1], E_["qfree", tt - 1],
                               E_["qexpS", tt - 1])
                            if (tt - 1) in (2, 4):
                                wt(q, E_["thr_psfree", tt - 1])
                    wt(q, E_["p1sx", tt, c])
                    q.matmul(out=PS[0][:, :], lhsT=ones128[:, :], rhs=xbfr[:, c % 2, :],
                             start=(c == 0), stop=(c == CE - 1))
                    i = q.matmul(out=PS[1][:, :], lhsT=ones128[:, :], rhs=xsqr[:, c % 2, :],
                                 start=(c == 0), stop=(c == CE - 1))
                    i = inc(i, "T")
                    E_["p1smm", tt, c] = i
                    if c == CE - 1:
                        E_["p1sums", tt] = i

                p1_slot_mm[c % 2] = E_["p1smm", tt, c]

            @block.vector
            def _(q, tt=tt):
                wt = WT["vector"]
                wt(q, E_["p1sums", tt])
                q.tensor_scalar(out=st[0][:, :], in0=PS[0][:, :], scalar1=1.0 / E,
                                scalar2=None, op0=ALU.mult)
                q.tensor_mul(out=st[1][:, :], in0=st[0][:, :], in1=st[0][:, :])
                i = q.scalar_tensor_tensor(out=st[2][:, :], in0=PS[1][:, :], scalar=1.0 / E,
                                           in1=st[1][:, :], op0=ALU.mult, op1=ALU.subtract)
                E_["p1facr", tt] = inc(i, "V")

            @block.scalar
            def _(q, tt=tt):
                wt = WT["scalar"]
                wt(q, E_["p1facr", tt])
                q.activation(out=st[1][:, :], in_=st[2][:, :], func=AF.Ln, bias=cst[:, 0:1])
                i = inc(q.activation(out=st[3][:, :], in_=st[1][:, :], func=AF.Exp,
                                     scale=cst[:, 1:2]), "S")
                E_["p1s", tt] = i

            @block.vector
            def _(q, tt=tt, buf=buf):
                wt = WT["vector"]
                wt(q, E_["p1s", tt])
                if tt > 0:
                    wt(q, E_["xn_used", tt - 1])
                for c in range(CE):
                    q.tensor_sub(out=st[1][:, :], in0=xa[:, buf, c, :], in1=st[0][:, :])
                    i = inc(q.tensor_mul(out=xn[:, c, :], in0=st[1][:, :], in1=st[3][:, :]),
                            "V")
                    E_["xnc", tt, c] = i
                E_["xn", tt] = i
                E_["xa_free", tt] = i
        for t in range(NT):
            buf = t % 2
            t0 = t * TT

            if t + 1 < NT:
                @block.sync
                def _(q, t=t, t0=t0):
                    wt = WT["sync"]
                    if t - 1 >= 0:
                        wt(q, E_["xa_free", t - 1])
                    for c in range(CE):
                        i = inc(q.dma_start(out=xa[:, (t + 1) % 2, c, :],
                                            in_=xT[c * P:(c + 1) * P, t0 + TT:t0 + 2 * TT]),
                                f"DX{(t + 1) % 2}", 16)
                    E_["x", t + 1] = i

            if t == 0:
                emit_ln1(0)
            pend_smm = [None]

            for j in range(NSUB):
                @block.tensor
                def _(q, t=t, j=j):
                    wt = WT["tensor"]
                    if t == 0 and j == 0:
                        wt(q, E_["cast_wk"], E_["cast_wv"])
                    pj = (t, j - 1) if j > 0 else ((t - 1, NSUB - 1) if t > 0 else None)
                    if pj is not None:
                        wt(q, E_["kminV", pj[0], pj[1]], E_["kreluS", pj[0], pj[1]], E_["vfree", pj[0], pj[1]])
                    for c in range(CE):
                        wt(q, E_["xnc", t, c])
                        lt = xn[:, c, j * P:(j + 1) * P]
                        q.matmul(out=PS[1][:, :], lhsT=lt, rhs=wk_sb[:, c, 0:TT],
                                 start=(c == 0), stop=False)
                        q.matmul(out=PS[2][:, :], lhsT=lt, rhs=wk_sb[:, c, TT:E],
                                 start=(c == 0), stop=False)
                        q.matmul(out=PS[3][:, :], lhsT=lt, rhs=wv_sb[:, c, 0:TT],
                                 start=(c == 0), stop=False)
                        q.matmul(out=PS[4][:, :], lhsT=lt, rhs=wv_sb[:, c, TT:E],
                                 start=(c == 0), stop=False)
                    q.matmul(out=PS[1][:, :], lhsT=ones1[0:1, :], rhs=wkrow[0:1, 0:TT],
                             start=False, stop=True)
                    q.matmul(out=PS[2][:, :], lhsT=ones1[0:1, :], rhs=wkrow[0:1, TT:E],
                             start=False, stop=True)
                    q.matmul(out=PS[3][:, :], lhsT=ones1[0:1, :], rhs=wvrow[0:1, 0:TT],
                             start=False, stop=True)
                    i = q.matmul(out=PS[4][:, :], lhsT=ones1[0:1, :], rhs=wvrow[0:1, TT:E],
                                 start=False, stop=True)
                    E_["kv", t, j] = inc(i, "T")

                smm_prev = E_.get(("smm", t - 1, j))
                prev_paS = [pa_lastS[0], pa_lastS[1]]
                prev_pcV = [pc_lastV[0], pc_lastV[1]]

                @block.vector
                def _(q, t=t, j=j, smm_prev=smm_prev, prev_paS=prev_paS):
                    wt = WT["vector"]
                    wt(q, E_["kv", t, j])
                    if smm_prev is not None:
                        wt(q, smm_prev)
                    vv = vext.ap().rearrange("p j (c k) -> p j c k", k=129)
                    q.tensor_copy(out=vv[:, j, 0:4, 0:128],
                                  in_=PS[3].ap().rearrange("p (c k) -> p c k", k=128))
                    i = inc(q.tensor_copy(out=vv[:, j, 4:8, 0:128],
                                          in_=PS[4].ap().rearrange("p (c k) -> p c k", k=128)),
                            "V")
                    E_["vev", t, j] = i
                    E_["vfree", t, j] = i
                    if prev_paS[0] is not None:
                        wt(q, prev_paS[0], prev_paS[1])
                    q.tensor_scalar(out=pa[:, 0, :], in0=PS[1][:, :], scalar1=0.0,
                                    scalar2=None, op0=ALU.min)
                    i = inc(q.tensor_scalar(out=pa[:, 1, :], in0=PS[2][:, :], scalar1=0.0,
                                            scalar2=None, op0=ALU.min), "V")
                    E_["kminV", t, j] = i

                @block.scalar
                def _(q, t=t, j=j, prev_pcV=prev_pcV):
                    wt = WT["scalar"]
                    wt(q, E_["kminV", t, j])
                    q.activation(out=pa[:, 0, :], in_=pa[:, 0, :], func=AF.Exp)
                    q.activation(out=pa[:, 1, :], in_=pa[:, 1, :], func=AF.Exp)
                    if prev_pcV[0] is not None:
                        wt(q, prev_pcV[0], prev_pcV[1])
                    q.activation(out=pc[:, 0, :], in_=PS[1][:, :], func=AF.Relu)
                    i = inc(q.activation(out=pc[:, 1, :], in_=PS[2][:, :], func=AF.Relu), "S")
                    E_["kreluS", t, j] = i

                @block.vector
                def _(q, t=t, j=j):
                    wt = WT["vector"]
                    wt(q, E_["kreluS", t, j])
                    q.tensor_add(out=phiK[:, j, 0:TT], in0=pa[:, 0, :], in1=pc[:, 0, :])
                    i = inc(q.tensor_add(out=phiK[:, j, TT:E], in0=pa[:, 1, :],
                                         in1=pc[:, 1, :]), "V")
                    E_["phik", t, j] = i

                pa_lastS[0] = E_["kreluS", t, j]
                pa_lastS[1] = E_["kreluS", t, j]
                pc_lastV[0] = E_["phik", t, j]
                pc_lastV[1] = E_["phik", t, j]

                def emit_smm(t=t, j=j):
                    @block.tensor
                    def _(q, t=t, j=j):
                        wt = WT["tensor"]
                        wt(q, E_["phik", t, j], E_["vev", t, j])
                        sp_ = (t == NT - 1 and j == NSUB - 1)
                        for c in range(CE):
                            st_ = (t == 0 and j == 0 and c % 3 == 0)
                            i = q.matmul(out=s_slice(c),
                                         lhsT=phiK[:, j, c * P:(c + 1) * P],
                                         rhs=vext[:, j, c * 129:(c + 1) * 129],
                                         start=st_, stop=sp_)
                        E_["smm", t, j] = inc(i, "T")

                if pend_smm[0] is not None:
                    pend_smm[0]()
                pend_smm[0] = emit_smm

                emit_tern_some(t, 2)

            if pend_smm[0] is not None:
                pend_smm[0]()
                pend_smm[0] = None

            if t == NT - 1:
                @block.vector
                def _(q):
                    wt = WT["vector"]
                    wt(q, E_["smm", NT - 1, NSUB - 1])
                    for c in range(CE):
                        i = q.tensor_copy(out=S_sb[:, c * 129:(c + 1) * 129],
                                          in_=s_slice(c))
                    for c in range(CE):
                        q.memset(S_sb[0:64, c * 129 + 64:c * 129 + 128], 0.0)
                        i = q.memset(S_sb[64:128, c * 129:c * 129 + 64], 0.0)
                    E_["sev"] = inc(i, "V")

                @block.gpsimd
                def _(q):
                    wt = WT["gpsimd"]
                    wt(q, E_["sev"])
                    i = inc(q.dma_start(out=cc_in, in_=S_sb[:, :]), "GS", 16)
                    wt(q, i)
                    i = inc(q.collective_compute(
                        "AllReduce", ALU.add,
                        replica_groups=[[0, 1, 2, 3], [4, 5, 6, 7]],
                        ins=[cc_in], outs=[cc_out]), "GS")
                    wt(q, i)
                    i = inc(q.dma_start(out=S_sb[:, :], in_=cc_out), "GS", 16)
                    E_["sx"] = i

                @block.vector
                def _(q):
                    wt = WT["vector"]
                    wt(q, E_["sx"])
                    q.tensor_copy(out=Sbf[:, :], in_=S_sb[:, :])
                    for c in range(CE):
                        q.tensor_copy(out=Zm[0:64, c, 2 * c:2 * c + 1],
                                      in_=S_sb[0:64, c * 129 + 128:c * 129 + 129])
                        i = q.tensor_copy(out=Zm[64:128, c, 2 * c + 1:2 * c + 2],
                                          in_=S_sb[64:128, c * 129 + 128:c * 129 + 129])
                    E_["szm"] = inc(i, "V")

            for c in range(CE):
                pq = PS[0] if c % 2 == 0 else PS[4]

                @block.tensor
                def _(q, t=t, c=c, pq=pq):
                    wt = WT["tensor"]
                    if t == 0 and c == 0:
                        wt(q, E_["cast_wq"])
                    if c >= 2:
                        wt(q, E_["qfreeV", t, c - 2], E_["qfreeS", t, c - 2])
                    elif c == 1:
                        wt(q, E_["vfree", t, NSUB - 1])
                    for e in range(CE):
                        i = q.matmul(out=pq[:, :], lhsT=wq_sb[:, e, c * P:(c + 1) * P],
                                     rhs=xn[:, e, :], start=(e == 0), stop=(e == CE - 1))
                    E_["qmm", t, c] = inc(i, "T")
                    if c == CE - 1:
                        E_["xn_used", t] = E_["qmm", t, c]

                extra = [E_["bias"]] if (t == 0 and c == 0) else None
                gprev = E_.get(("phiq_out", t - 1))
                extra_out = [gprev] if (c == 0 and gprev is not None) else None
                mq, rq, tq = emit_phi(c % 2, pq[:, :], bq_sb[:, c:c + 1],
                                      phiq_sb[:, c, :], E_["qmm", t, c],
                                      extra_wV=extra, extra_out_wV=extra_out)
                E_["qfreeV", t, c] = mq
                E_["qfreeS", t, c] = rq
                E_["phiq_tt", t, c] = tq
                if c % 2 == 0:
                    emit_tern_some(t, 2)

            E_["qfree", t] = E_["phiq_tt", t, CE - 1]
            E_["qexpS", t] = E_["qfreeS", t, CE - 1]

            @block.gpsimd
            def _(q, t=t, t0=t0):
                wt = WT["gpsimd"]
                wt(q, E_["phiq_tt", t, CE - 1])
                i = inc(q.dma_start(
                    out=phiq_d[:, t0:t0 + TT].rearrange("(c p) tt -> p c tt", p=P),
                    in_=phiq_sb[:, :, :]), "GQ", 16)
                E_["phiq_out", t] = i



            if t in (2, 4):
                which = 0 if t == 2 else 1
                col = slice(0, 1) if which == 0 else slice(1, 2)
                thr_t = thrU if which == 0 else thrD
                red_ev = E_["alup_red", 15] if which == 0 else E_["aldn_red", 31]
                parts_t = alpp if which == 0 else alpd

                nw = {0: 16, 1: CF}[which]

                @block.vector
                def _(q, t=t, red_ev=red_ev, parts_t=parts_t, nw=nw):
                    wt = WT["vector"]
                    wt(q, red_ev)
                    w2 = nw // 2
                    i = inc(q.tensor_add(out=alptree[:, 0:w2], in0=parts_t[:, 0:w2],
                                         in1=parts_t[:, w2:nw]), "V")
                    while w2 > 1:
                        wt(q, i)
                        w2 //= 2
                        i = inc(q.tensor_add(out=alptree[:, 0:w2], in0=alptree[:, 0:w2],
                                             in1=alptree[:, w2:2 * w2]), "V")
                    E_["altot", t] = i

                @block.gpsimd
                def _(q, t=t):
                    wt = WT["gpsimd"]
                    wt(q, E_["altot", t])
                    i = inc(q.dma_start(out=row128[0:1, 0:P], in_=alptree[:, 0:1],
                                        allow_slow_non_contiguous=True), "GF", 16)
                    E_["alrow", t] = i

                @block.vector
                def _(q, t=t):
                    wt = WT["vector"]
                    wt(q, E_["alrow", t])
                    w2 = P // 2
                    i = inc(q.tensor_add(out=row128[0:1, 0:w2], in0=row128[0:1, 0:w2],
                                         in1=row128[0:1, w2:P]), "V")
                    while w2 > 1:
                        wt(q, i)
                        w2 //= 2
                        i = inc(q.tensor_add(out=row128[0:1, 0:w2], in0=row128[0:1, 0:w2],
                                             in1=row128[0:1, w2:2 * w2]), "V")
                    wt(q, i)
                    q.tensor_scalar(out=throw[0:1, 0:1], in0=row128[0:1, 0:1],
                                    scalar1=0.5 / (DF * E), scalar2=None, op0=ALU.mult)
                    i = inc(q.tensor_scalar(out=throw[0:1, 1:2], in0=row128[0:1, 0:1],
                                            scalar1=-0.5 / (DF * E), scalar2=None,
                                            op0=ALU.mult), "V")
                    E_["throw", t] = i

                @block.tensor
                def _(q, t=t):
                    wt = WT["tensor"]
                    wt(q, E_["throw", t], E_["qfreeV", t, CE - 2], E_["qfreeS", t, CE - 2])
                    i = q.matmul(out=PS[0][:, 0:2], lhsT=ones1f[0:1, :], rhs=throw[0:1, 0:2],
                                 start=True, stop=True)
                    E_["thrbc_mm", t] = inc(i, "T")

                @block.vector
                def _(q, t=t, thr_t=thr_t, which=which):
                    wt = WT["vector"]
                    wt(q, E_["thrbc_mm", t])
                    i = inc(q.tensor_copy(out=thr_t[:, :], in_=PS[0][:, 0:2]), "V")
                    E_["thr_done", which] = i
                    E_["thr_psfree", t] = i

            emit_tern_some(t, 16)

            if t + 1 < NT:
                emit_ln1(t + 1)

        for hh in range(2):
            @block.scalar
            def _(q, hh=hh):
                wt = WT["scalar"]
                wt(q, E_["qmm", NT - 1, CE - 1])
                if hh == 1:
                    wt(q, E_["wo_cast", 0])
                i = inc(q.dma_start(
                    out=wostag[:, :, :],
                    in_=woT[hh * 4 * P:(hh + 1) * 4 * P, :].rearrange("(c p) f -> p c f", p=P)),
                    "AW", 16)
                E_["wo_ld", hh] = i

            @block.vector
            def _(q, hh=hh):
                wt = WT["vector"]
                wt(q, E_["wo_ld", hh])
                i = inc(q.tensor_copy(
                    out=wo_sb[:, hh * 4:(hh + 1) * 4, :].rearrange("p c e -> p (c e)"),
                    in_=wostag[:, :, :].rearrange("p c e -> p (c e)")), "V")
                E_["wo_cast", hh] = i
        E_["wo"] = E_["wo_cast", 1]

        def ctx_ap(c):
            return phiK[:, c // 2, (c % 2) * TT:(c % 2 + 1) * TT]

        for t in range(NT):
            buf = t % 2
            t0 = t * TT

            @block.gpsimd
            def _(q, t=t, t0=t0):
                wt = WT["gpsimd"]
                if t == 0:
                    wt(q, E_["phiq_out", NT - 1])
                else:
                    wt(q, E_["nd_used", t - 1])
                i = inc(q.dma_start(
                    out=phiq_sb[:, :, :],
                    in_=phiq_d[:, t0:t0 + TT].rearrange("(c p) tt -> p c tt", p=P)),
                    "GQ", 16)
                E_["phiq_in", t] = i

            @block.sync
            def _(q, t=t, t0=t0, buf=buf):
                wt = WT["sync"]
                if t >= 2:
                    wt(q, E_["r_done", t - 2])
                else:
                    wt(q, E_["xa_free", NT - 2 + t])
                for c in range(CE):
                    i = inc(q.dma_start(out=xa[:, buf, c, :],
                                        in_=xT[c * P:(c + 1) * P, t0:t0 + TT]), f"DX{buf}", 16)
                E_["x2", t] = i

            def emit_upld(g, t=t):
                @block.scalar
                def _(q, g=g, t=t):
                    wt = WT["scalar"]
                    if (t, g) == (0, 0):
                        wt(q, E_["tu_done"], E_["td_done"])
                    prev = (t, g - 2) if g >= 2 else ((t - 1, g + 6) if t > 0 else None)
                    if prev is not None:
                        wt(q, E_["upmm", prev[0], prev[1]])
                    i = inc(q.dma_start(
                        out=upbuf[:, g % 2, :, :],
                        in_=wupt_d.rearrange("(c p) f -> p c f", p=P)[:, :, g * TT:(g + 1) * TT]),
                        f"AU{g % 2}", 16)
                    E_["upld", t, g] = i

            def emit_dnld(dc, t=t):
                @block.scalar
                def _(q, dc=dc, t=t):
                    wt = WT["scalar"]
                    if (t, dc) == (0, 0):
                        wt(q, E_["td_done"])
                    if t == 0 and dc < 2:
                        wt(q, E_["wo"])
                    prev = (t, dc - 2) if dc >= 2 else ((t - 1, dc + 6) if t > 0 else None)
                    if prev is not None:
                        wt(q, E_["dnmm", prev[0], prev[1]])
                    i = inc(q.dma_start(out=dnbuf[:, dc % 2, 0:16, :],
                                        in_=wdnt_d[dc, :, 0:16, :]), f"AE{dc % 2}", 16)
                    E_["dnldA", t, dc] = i
                    i = inc(q.dma_start(out=dnbuf[:, dc % 2, 16:32, :],
                                        in_=wdnt_d[dc, :, 16:32, :]), f"AD{dc % 2}", 16)
                    E_["dnld", t, dc] = i

            emit_upld(0)
            emit_upld(1)
            emit_dnld(0)
            emit_dnld(1)

            @block.tensor
            def _(q, t=t):
                wt = WT["tensor"]
                wt(q, E_["phiq_in", t], E_["szm"])
                if t > 0:
                    wt(q, E_["recv", t - 1], E_["gelu", t - 1, 29])
                for c in range(CE):
                    i = q.matmul(out=PS[1][0:H, :], lhsT=Zm[:, c, :], rhs=phiq_sb[:, c, :],
                                 start=(c == 0), stop=(c == CE - 1))
                E_["denmm", t] = inc(i, "T")

            @block.vector
            def _(q, t=t):
                wt = WT["vector"]
                wt(q, E_["denmm", t])
                q.tensor_scalar(out=den16[0:H, :], in0=PS[1][0:H, :], scalar1=EPS_ATT,
                                scalar2=None, op0=ALU.add)
                q.reciprocal(out=den16[0:H, :], in_=den16[0:H, :])
                i = inc(q.tensor_copy(out=rec16b[0:H, :], in_=den16[0:H, :]), "V")
                E_["recv", t] = i

            for c in range(CE):
                pn = PS[0] if c % 2 == 0 else PS[4]
                pe_ = PS[2] if c % 2 == 0 else PS[6]

                @block.tensor
                def _(q, t=t, c=c, pn=pn, pe_=pe_):
                    wt = WT["tensor"]
                    wt(q, E_["phiq_in", t], E_["szm"])
                    if c >= 2:
                        wt(q, E_["ctxv", t, c - 2])
                    elif t > 0:
                        wt(q, E_["gelu", t - 1, 30 + c])
                        if c == 0:
                            wt(q, E_["outv", t - 1, 5])
                    i = q.matmul(out=pn[:, :], lhsT=Sbf[:, c * 129:c * 129 + 128],
                                 rhs=phiq_sb[:, c, :], start=True, stop=True)
                    E_["nummm", t, c] = inc(i, "T")
                    wt(q, E_["recv", t])
                    if c >= 2:
                        wt(q, E_["dbs", t, c - 2])
                    elif t > 0:
                        if c == 0:
                            wt(q, E_["b2ev", t - 1])
                    i = q.matmul(out=pe_[:, :], lhsT=Ec[0:H, c, :], rhs=rec16b[0:H, :],
                                 start=True, stop=True)
                    E_["ebc", t, c] = inc(i, "T")
                    if c == CE - 1:
                        E_["nd_used", t] = ev("T")

                @block.vector
                def _(q, t=t, c=c, pe_=pe_):
                    wt = WT["vector"]
                    wt(q, E_["ebc", t, c])
                    i = inc(q.tensor_copy(out=dbs_ap(c), in_=pe_[:, :]), "V")
                    E_["dbs", t, c] = i

                @block.vector
                def _(q, t=t, c=c, pn=pn):
                    wt = WT["vector"]
                    wt(q, E_["nummm", t, c], E_["dbs", t, c])
                    i = inc(q.tensor_mul(out=ctx_ap(c), in0=pn[:, :],
                                         in1=dbs_ap(c)), "V")
                    E_["ctxv", t, c] = i

            for c in range(CE):
                pat = PS[3] if c % 2 == 0 else PS[5]

                @block.tensor
                def _(q, t=t, c=c, pat=pat):
                    wt = WT["tensor"]
                    wt(q, E_["ctxv", t, CE - 1], E_["wo"])
                    if c >= 2:
                        wt(q, E_["rstt", t, c - 2])
                    elif t > 0:
                        wt(q, E_["outv", t - 1, 6 + c])
                    for e in range(CE):
                        i = q.matmul(out=pat[:, :], lhsT=wo_sb[:, e, c * P:(c + 1) * P],
                                     rhs=ctx_ap(e), start=(e == 0), stop=(e == CE - 1))
                    E_["attn", t, c] = inc(i, "T")

                @block.vector
                def _(q, t=t, c=c, pat=pat, buf=buf):
                    wt = WT["vector"]
                    wt(q, E_["attn", t, c], E_["x2", t])
                    if t == 0 and c == 0:
                        wt(q, E_["qfreeS", NT - 1, CE - 1])
                    i = inc(q.scalar_tensor_tensor(out=r_ap(c), in0=pat[:, :],
                                                   scalar=bo_sb[:, c:c + 1],
                                                   in1=xa[:, buf, c, :],
                                                   op0=ALU.add, op1=ALU.add), "V")
                    E_["rstt", t, c] = i

            E_["r_done", t] = E_["rstt", t, CE - 1]

            for c in range(CE):
                prev_slot = p1_slot_mm[c % 2]

                @block.vector
                def _(q, t=t, c=c, prev_slot=prev_slot):
                    wt = WT["vector"]
                    wt(q, E_["rstt", t, c])
                    if prev_slot is not None:
                        wt(q, prev_slot)
                    q.tensor_copy(out=xbfr[:, c % 2, :], in_=r_ap(c))
                    i = inc(q.tensor_mul(out=xsqr[:, c % 2, :], in0=xbfr[:, c % 2, :],
                                         in1=xbfr[:, c % 2, :]), "V")
                    E_["p2sx", t, c] = i

                @block.tensor
                def _(q, t=t, c=c):
                    wt = WT["tensor"]
                    if c == 0:
                        wt(q, E_["dbs", t, CE - 1])
                        if t > 0:
                            wt(q, E_["p2facr", t - 1])
                    wt(q, E_["p2sx", t, c])
                    q.matmul(out=PS[6][:, :], lhsT=ones128[:, :], rhs=xbfr[:, c % 2, :],
                             start=(c == 0), stop=(c == CE - 1))
                    i = q.matmul(out=PS[7][:, :], lhsT=ones128[:, :], rhs=xsqr[:, c % 2, :],
                                 start=(c == 0), stop=(c == CE - 1))
                    i = inc(i, "T")
                    E_["p2smm", t, c] = i
                    if c == CE - 1:
                        E_["p2sums", t] = i

                p1_slot_mm[c % 2] = E_["p2smm", t, c]

            @block.vector
            def _(q, t=t):
                wt = WT["vector"]
                wt(q, E_["p2sums", t])
                q.tensor_scalar(out=st[0][:, :], in0=PS[6][:, :], scalar1=1.0 / E,
                                scalar2=None, op0=ALU.mult)
                q.tensor_mul(out=st[1][:, :], in0=st[0][:, :], in1=st[0][:, :])
                i = q.scalar_tensor_tensor(out=st[2][:, :], in0=PS[7][:, :], scalar=1.0 / E,
                                           in1=st[1][:, :], op0=ALU.mult, op1=ALU.subtract)
                E_["p2facr", t] = inc(i, "V")

            @block.scalar
            def _(q, t=t):
                wt = WT["scalar"]
                wt(q, E_["p2facr", t])
                q.activation(out=st[1][:, :], in_=st[2][:, :], func=AF.Sqrt, bias=cst[:, 4:5])
                i = inc(q.activation(out=st[2][:, :], in_=st[2][:, :], func=AF.Sqrt,
                                     bias=cst[:, 0:1]), "S")
                E_["p2sq", t] = i

            @block.vector
            def _(q, t=t):
                wt = WT["vector"]
                wt(q, E_["p2sq", t])
                q.scalar_tensor_tensor(out=st[3][:, :], in0=st[2][:, :], scalar=EPS_RMS,
                                       in1=st[1][:, :], op0=ALU.mult, op1=ALU.add)
                q.reciprocal(out=st[3][:, :], in_=st[3][:, :])
                if t > 0:
                    wt(q, E_["xn2_used", t - 1])
                for c in range(CE):
                    q.tensor_sub(out=st[1][:, :], in0=r_ap(c), in1=st[0][:, :])
                    i = inc(q.tensor_mul(out=xn[:, c, :], in0=st[1][:, :], in1=st[3][:, :]),
                            "V")
                    E_["xn2c", t, c] = i
                E_["xn2", t] = i

            UPPS = [PS[0], PS[4], PS[1]]

            def emit_h2(fc, t=t):
                @block.tensor
                def _(q, t=t, fc=fc):
                    wt = WT["tensor"]
                    wt(q, E_["hsq", t, fc])
                    if t > 0 and fc == 0:
                        wt(q, E_["b2ev", t - 1])
                    i = q.matmul(out=PS[2][:, :], lhsT=ones128[:, :],
                                 rhs=hsqr[:, fc % 4, :],
                                 start=(fc == 0), stop=(fc == CF - 1))
                    E_["h2mm", t, fc] = inc(i, "T")

            for g in range(CE):
                if g >= 2:
                    emit_upld(g)
                for fo in range(4):
                    fc = g * 4 + fo
                    pu = UPPS[fc % 3]

                    @block.tensor
                    def _(q, t=t, g=g, fc=fc, pu=pu):
                        wt = WT["tensor"]
                        wt(q, E_["upld", t, g])
                        if fc >= 3:
                            wt(q, E_["gelu", t, fc - 3])
                        elif fc == 2:
                            wt(q, E_["recv", t])
                            if t > 0:
                                wt(q, E_["gelu", t - 1, 29])
                        for e in range(CE):
                            wt(q, E_["xn2c", t, e])
                            i = q.matmul(out=pu[:, :],
                                         lhsT=upbuf[:, g % 2, e, (fc % 4) * P:(fc % 4 + 1) * P],
                                         rhs=xn[:, e, :], start=(e == 0), stop=(e == CE - 1))
                        E_["upmmc", t, fc] = inc(i, "T")
                        if fc % 4 == 3:
                            E_["upmm", t, g] = ev("T")
                        if fc == CF - 1:
                            E_["xn2_used", t] = ev("T")
                            E_["up_psfree", t] = ev("T")

                    if fc >= 2:
                        emit_h2(fc - 2)

                    @block.scalar
                    def _(q, t=t, fc=fc, pu=pu):
                        wt = WT["scalar"]
                        wt(q, E_["upmmc", t, fc])
                        i = inc(q.activation(out=h_sb[:, fc, :], in_=pu[:, :], func=AF.Gelu,
                                             scale=cst[:, 2:3]), "S")
                        E_["gelu", t, fc] = i

                    @block.vector
                    def _(q, t=t, fc=fc):
                        wt = WT["vector"]
                        wt(q, E_["gelu", t, fc])
                        prev = (t, fc - 4) if fc >= 4 else ((t - 1, fc + CF - 4) if t > 0 else None)
                        if prev is not None:
                            wt(q, E_["h2mm", prev[0], prev[1]])
                        i = inc(q.tensor_mul(out=hsqr[:, fc % 4, :], in0=h_sb[:, fc, :],
                                             in1=h_sb[:, fc, :]), "V")
                        E_["hsq", t, fc] = i

            emit_h2(CF - 2)
            emit_h2(CF - 1)

            @block.scalar
            def _(q, t=t):
                wt = WT["scalar"]
                wt(q, E_["h2mm", t, CF - 1])
                i = inc(q.activation(out=b2_sb[:, :], in_=PS[2][:, :], func=AF.Sqrt,
                                     scale=cst[:, 3:4]), "S")
                E_["b2ev", t] = i

            @block.vector
            def _(q, t=t):
                wt = WT["vector"]
                wt(q, E_["b2ev", t])
                q.tensor_scalar(out=b2_sb[:, :], in0=b2_sb[:, :], scalar1=2.0 * EPS_RMS,
                                scalar2=None, op0=ALU.add)
                i = inc(q.reciprocal(out=b2_sb[:, :], in_=b2_sb[:, :]), "V")
                E_["rb", t] = i

            for dc in range(CE):
                if dc >= 2:
                    emit_dnld(dc)
                pd = [PS[3], PS[5], PS[0]][dc % 3]

                @block.tensor
                def _(q, t=t, dc=dc, pd=pd):
                    wt = WT["tensor"]
                    wt(q, E_["gelu", t, CF - 1], E_["dnldA", t, dc])
                    if dc >= 3:
                        wt(q, E_["outv", t, dc - 3])
                    elif dc < 2:
                        wt(q, E_["rstt", t, 6 + dc])
                    for fc in range(CF):
                        if fc == 16:
                            wt(q, E_["dnld", t, dc])
                        i = q.matmul(out=pd[:, :], lhsT=dnbuf[:, dc % 2, fc, :],
                                     rhs=h_sb[:, fc, :], start=(fc == 0), stop=(fc == CF - 1))
                    E_["dnmm", t, dc] = inc(i, "T")

                @block.vector
                def _(q, t=t, dc=dc, pd=pd):
                    wt = WT["vector"]
                    wt(q, E_["dnmm", t, dc], E_["rb", t])
                    prev = (t, dc - 2) if dc >= 2 else ((t - 1, dc + 6) if t > 0 else None)
                    if prev is not None:
                        wt(q, E_["ost", prev[0], prev[1]])
                    q.tensor_mul(out=t4[:, :], in0=pd[:, :], in1=b2_sb[:, :])
                    i = inc(q.tensor_add(out=out_sb[:, dc % 2, :], in0=t4[:, :],
                                         in1=r_ap(dc)), "V")
                    E_["outv", t, dc] = i

                @block.sync
                def _(q, t=t, dc=dc, t0=t0):
                    wt = WT["sync"]
                    wt(q, E_["outv", t, dc])
                    i = inc(q.dma_start(out=outT[dc * P:(dc + 1) * P, t0:t0 + TT],
                                        in_=out_sb[:, dc % 2, :]), f"DO{dc % 2}", 16)
                    E_["ost", t, dc] = i

        @block.sync
        def _(q):
            q.wait_ge(sem["DO0"], cnt["DO0"])
            q.wait_ge(sem["DO1"], cnt["DO1"])

        import os
        if os.environ.get("KBUILD_SYNCCHECK"):
            semname = {id(h): n for n, h in sem.items()}

            class _RecQ:
                def __init__(self, items):
                    self.items = items

                def wait_ge(self, h, v):
                    self.items.append(("wait", semname[id(h)], v))

                def __getattr__(self, name):
                    items = self.items

                    class _RI:
                        def __init__(self):
                            items.append(("instr", name, []))
                            self.slot = items[-1]

                        def then_inc(self, h, n):
                            self.slot[2].append((semname[id(h)], n))
                            return self

                    def f(*a, **k):
                        return _RI()
                    return f

            streams = {}
            for s in cnt:
                cnt[s] = 0
            waited.clear()
            for ename in ENGINES:
                items = []
                rq = _RecQ(items)
                for fn in SB.lists[ename]:
                    fn(rq)
                streams[ename] = items
            return streams

        for s in cnt:
            cnt[s] = 0
        waited.clear()
        with nc.Block() as rb:
            for ename in ENGINES:
                fns = SB.lists[ename]

                def flush(q, fns=fns):
                    for fn in fns:
                        fn(q)

                getattr(rb, ename)(flush)

    return nc


_NC_CACHE = {}


def kernel(L0_attn_query, L0_attn_Wq, L0_attn_bq, L0_attn_Wk, L0_attn_bk,
           L0_attn_Wv, L0_attn_bv, L0_attn_Wo, L0_attn_bo,
           L0_ffn_w_up, L0_ffn_w_down):
    x = np.asarray(L0_attn_query)
    B, L, _ = x.shape
    wqT_h = np.ascontiguousarray(np.asarray(L0_attn_Wq).T)
    wkTe_h = np.ascontiguousarray(np.concatenate(
        [np.asarray(L0_attn_Wk).T, np.asarray(L0_attn_bk)[None, :]], axis=0))
    wvTe_h = np.ascontiguousarray(np.concatenate(
        [np.asarray(L0_attn_Wv).T, np.asarray(L0_attn_bv)[None, :]], axis=0))
    woT_h = np.ascontiguousarray(np.asarray(L0_attn_Wo).T)
    wupT_h = np.ascontiguousarray(np.asarray(L0_ffn_w_up).T)
    wdnT_h = np.ascontiguousarray(np.asarray(L0_ffn_w_down).T)
    bq_h = np.ascontiguousarray(np.asarray(L0_attn_bq))
    bo_h = np.ascontiguousarray(np.asarray(L0_attn_bo))

    selE = np.zeros((H, CE * P), dtype=np.float32)
    for c in range(CE):
        selE[2 * c, c * P:c * P + 64] = 1.0
        selE[2 * c + 1, c * P + 64:(c + 1) * P] = 1.0

    if "nc" not in _NC_CACHE:
        _NC_CACHE["nc"] = build_program()
    nc = _NC_CACHE["nc"]

    in_maps = []
    for c in range(NCORE):
        b, qr = c // 4, c % 4
        xs = np.ascontiguousarray(x[b, qr * TPC:(qr + 1) * TPC, :].T)
        in_maps.append({
            "xT": xs, "wqT": wqT_h, "wkTe": wkTe_h, "wvTe": wvTe_h, "woT": woT_h,
            "bq": bq_h, "bo": bo_h, "wupT": wupT_h, "wdnT": wdnT_h, "selE": selE,
        })

    res = run_bass_kernel_spmd(nc, in_maps, core_ids=list(range(NCORE)))
    out = np.empty((B, L, E), dtype=np.float32)
    for c in range(NCORE):
        b, qr = c // 4, c % 4
        out[b, qr * TPC:(qr + 1) * TPC, :] = res.results[c]["outT"].T
    return out


# version 24 (final)
# speedup vs baseline: 1.0067x; 1.0067x over previous
import contextlib
import numpy as np

import concourse.bass as bass
import concourse.mybir as mybir
from concourse.bass_utils import run_bass_kernel_spmd

F32 = mybir.dt.float32
BF16 = mybir.dt.bfloat16
AF = mybir.ActivationFunctionType
ALU = mybir.AluOpType
AX = mybir.AxisListType

P = 128
E = 1024
CE = 8
DF = 4096
CF = 32
H = 16
TT = 512
NT = 8
TPC = TT * NT
NSUB = 4
NCORE = 8

EPS_LN = 1e-5
EPS_ATT = 1e-6
EPS_RMS = 1e-8


def build_program():
    nc = bass.Bass("TRN2", target_bir_lowering=False)

    xT = nc.dram_tensor("xT", [E, TPC], F32, kind="ExternalInput").ap()
    wqT = nc.dram_tensor("wqT", [E, E], F32, kind="ExternalInput").ap()
    wkTe = nc.dram_tensor("wkTe", [E + 1, E], F32, kind="ExternalInput").ap()
    wvTe = nc.dram_tensor("wvTe", [E + 1, E], F32, kind="ExternalInput").ap()
    woT = nc.dram_tensor("woT", [E, E], F32, kind="ExternalInput").ap()
    bq_d = nc.dram_tensor("bq", [E], F32, kind="ExternalInput").ap()
    bo_d = nc.dram_tensor("bo", [E], F32, kind="ExternalInput").ap()
    wupT = nc.dram_tensor("wupT", [E, DF], F32, kind="ExternalInput").ap()
    selE_d = nc.dram_tensor("selE", [H, CE * P], F32, kind="ExternalInput").ap()
    wdnT = nc.dram_tensor("wdnT", [DF, E], F32, kind="ExternalInput").ap()
    outT = nc.dram_tensor("outT", [E, TPC], F32, kind="ExternalOutput").ap()

    phiq_d = nc.dram_tensor("phiq_d", [E, TPC], BF16, kind="Internal").ap()
    wupt_d = nc.dram_tensor("wupt_d", [E, DF], BF16, kind="Internal").ap()
    wdnt_d = nc.dram_tensor("wdnt_d", [CE, P, CF, P], BF16, kind="Internal").ap()
    cc_in = nc.dram_tensor("cc_in", [P, 1032], F32, kind="Internal").ap()
    cc_out = nc.dram_tensor("cc_out", [P, 1032], F32, kind="Internal").ap()

    A = nc.alloc_sbuf_tensor
    xa = A("xa", [P, 2, CE, TT], F32)
    xn = A("xn", [P, CE, TT], BF16)
    bq_sb = A("bq_sb", [P, CE], F32)
    bo_sb = A("bo_sb", [P, CE], F32)
    ones128 = A("ones128", [P, P], BF16)
    ones1 = A("ones1", [1, P], BF16)
    ones1f = A("ones1f", [1, P], F32)
    onescol = A("onescol", [P, 1], F32)
    cst = A("cst", [P, 8], F32)
    st = [A(f"st{i}", [P, TT], F32) for i in range(4)]
    xbfr = A("xbfr", [P, 2, TT], BF16)
    xsqr = A("xsqr", [P, 2, TT], BF16)
    hsqr = A("hsqr", [P, 4, TT], BF16)
    phiK = A("phiK", [P, NSUB, E], BF16)
    phiq_sb = A("phiq_sb", [P, CE, TT], BF16)
    Sbf = A("Sbf", [P, 8 * 129], BF16)
    Zm = A("Zm", [P, CE, H], BF16)
    Ec = A("Ec", [H, CE, P], BF16)
    rec16b = A("rec16b", [H, TT], BF16)
    den16 = A("den16", [H, TT], F32)
    alp = A("alp", [P, 2], F32)
    alpp = A("alpp", [P, 16], F32)
    alpd = A("alpd", [P, CF], F32)
    alptree = A("alptree", [P, H], F32)
    row128 = A("row128", [1, P], F32)
    s2t = A("s2t", [1, 1], F32)
    throw = A("throw", [1, 2], F32)
    thrU = A("thrU", [P, 2], F32)
    thrD = A("thrD", [P, 2], F32)
    t4 = A("t4", [P, TT], F32)
    b2_sb = A("b2_sb", [P, TT], F32)
    selE_f = A("selE_f", [H, CE * P], F32)

    TOP = 229344
    AT = nc.alloc_sbuf_tensor_at
    REG_A = TOP - 32768
    wk_sb = AT("wk_sb", [P, CE, E], BF16, offset=REG_A)
    wv_sb = AT("wv_sb", [P, CE, E], BF16, offset=REG_A + 16384)
    upbuf = AT("upbuf", [P, 2, CE, TT], BF16, offset=REG_A)
    dnbuf = AT("dnbuf", [P, 2, CF, P], BF16, offset=REG_A + 16384)
    wostag = AT("wostag", [P, 4, E], F32, offset=REG_A + 16384)

    REG_B = REG_A - 32768
    stag = AT("stag", [P, CE, E], F32, offset=REG_B)
    h_sb = AT("h_sb", [P, CF, TT], BF16, offset=REG_B)
    ternb = AT("ternb", [P, DF], F32, offset=REG_B)
    ternbh = AT("ternbh", [P, 3, E], F32, offset=REG_B)
    terno = AT("terno", [P, 3, E], BF16, offset=REG_B + 12288)
    tb2 = AT("tb2", [P, 3, E], BF16, offset=REG_B + 18432)
    adbuf = AT("adbuf", [P, 2, E], F32, offset=REG_B + 24576)

    REG_C = REG_B - 16384
    wq_sb = AT("wq_sb", [P, CE, E], BF16, offset=REG_C)
    wo_sb = AT("wo_sb", [P, CE, E], BF16, offset=REG_C)

    REG_D = REG_C - 16448
    vext = AT("vext", [P, NSUB, 8 * 129], BF16, offset=REG_D)
    pa = AT("pa", [P, 2, TT], F32, offset=REG_D + 8256)
    pc = AT("pc", [P, 2, TT], F32, offset=REG_D + 12352)
    r_sb = AT("r_sb", [P, CE, TT], F32, offset=REG_D)

    REG_E = REG_D - 4096
    wkrow = AT("wkrow", [1, E], BF16, offset=REG_E)
    wvrow = AT("wvrow", [1, E], BF16, offset=REG_E + 2048)
    dbs0 = AT("dbs0", [P, TT], F32, offset=REG_E)
    dbs1 = AT("dbs1", [P, TT], F32, offset=REG_E + 2048)

    REG_H = REG_E - 4128
    S_sb = AT("S_sb", [P, 8 * 129], F32, offset=REG_H)
    out_sb = AT("out_sb", [P, 2, TT], F32, offset=REG_H)

    def r_ap(c):
        return r_sb[:, c, :]

    def dbs_ap(c):
        return (dbs0 if c % 2 == 0 else dbs1)[:, :]


    PS = [nc.alloc_psum_tensor(f"ps{i}", [P, TT], F32) for i in range(8)]

    def s_slice(c):
        return PS[5 + c // 3][:, (c % 3) * 129:(c % 3) * 129 + 129]

    cnt = {}
    waited = {}
    E_ = {}
    ENGINES = ("sync", "scalar", "vector", "gpsimd", "tensor")

    class _MockInstr:
        def then_inc(self, *a, **k):
            return self

    class _MockQ:
        def __getattr__(self, name):
            def f(*a, **k):
                return _MockInstr()
            return f

    class _Sec:
        def __init__(self):
            self.lists = {e: [] for e in ENGINES}
            self.mock = _MockQ()
            self.real = False

        def sec(self, name):
            def deco(fn):
                fn(self.mock)
                self.lists[name].append(fn)
            return deco

    SB = _Sec()

    with contextlib.ExitStack() as stack:
        sem = {}
        for s in ("T", "V", "S",
                  "DX0", "DX1", "DC", "DO0", "DO1",
                  "AW", "AU0", "AU1", "AD0", "AD1",
                  "GQ", "GA", "GB0", "GB1", "GT0", "GT1", "GT2",
                  "GO0", "GO1", "GO2", "GS", "GF", "PP", "AE0", "AE1"):
            sem[s] = stack.enter_context(nc.semaphore(s))
            cnt[s] = 0

        def inc(instr, s, n=1):
            instr.then_inc(sem[s], n)
            cnt[s] += n
            return (s, cnt[s])

        def ev(s):
            return (s, cnt[s])

        def mkwt(ename):
            def wt(q, *evs):
                for e in evs:
                    if e is None:
                        continue
                    (s, v) = e
                    if v <= 0:
                        continue
                    key = (ename, s)
                    if waited.get(key, 0) < v:
                        q.wait_ge(sem[s], v)
                        waited[key] = v
            return wt

        WT = {n: mkwt(n) for n in ENGINES}

        class _BlockShim:
            def __getattr__(self, name):
                return SB.sec(name)

        block = _BlockShim()

        pa_lastS = [None, None]
        pc_lastV = [None, None]

        def emit_phi(slot, ps_ap, bias, out_ap, mm_ev, extra_wV=None, extra_out_wV=None):
            prev_paS = pa_lastS[slot]
            prev_pcV = pc_lastV[slot]

            @block.vector
            def _(q):
                wt = WT["vector"]
                wt(q, mm_ev)
                if prev_paS is not None:
                    wt(q, prev_paS)
                if extra_wV:
                    wt(q, *extra_wV)
                if bias is None:
                    i = inc(q.tensor_scalar(out=pa[:, slot, :], in0=ps_ap, scalar1=0.0,
                                            scalar2=None, op0=ALU.min), "V")
                else:
                    i = inc(q.tensor_scalar(out=pa[:, slot, :], in0=ps_ap, scalar1=bias,
                                            scalar2=0.0, op0=ALU.add, op1=ALU.min), "V")
                E_["_phimin"] = i

            minev = E_["_phimin"]

            @block.scalar
            def _(q):
                wt = WT["scalar"]
                wt(q, minev)
                q.activation(out=pa[:, slot, :], in_=pa[:, slot, :], func=AF.Exp)
                if prev_pcV is not None:
                    wt(q, prev_pcV)
                if bias is None:
                    i = inc(q.activation(out=pc[:, slot, :], in_=ps_ap, func=AF.Relu), "S")
                else:
                    i = inc(q.activation(out=pc[:, slot, :], in_=ps_ap, func=AF.Relu,
                                         bias=bias), "S")
                E_["_phirelu"] = i

            reluev = E_["_phirelu"]
            pa_lastS[slot] = reluev

            @block.vector
            def _(q):
                wt = WT["vector"]
                wt(q, reluev)
                if extra_out_wV:
                    wt(q, *extra_out_wV)
                i = inc(q.tensor_add(out=out_ap, in0=pa[:, slot, :], in1=pc[:, slot, :]), "V")
                E_["_phitt"] = i

            ttev = E_["_phitt"]
            pc_lastV[slot] = ttev
            return minev, reluev, ttev

        @block.sync
        def _(q):
            for c in range(CE):
                i = inc(q.dma_start(out=xa[:, 0, c, :], in_=xT[c * P:(c + 1) * P, 0:TT]),
                        "DX0", 16)
            E_["x", 0] = i
            i = inc(q.dma_start(out=bq_sb[:, :], in_=bq_d.rearrange("(c p) -> p c", p=P),
                                allow_slow_non_contiguous=True), "DC", 16)
            i = inc(q.dma_start(out=bo_sb[:, :], in_=bo_d.rearrange("(c p) -> p c", p=P),
                                allow_slow_non_contiguous=True), "DC", 16)
            E_["bias"] = i
            i = inc(q.dma_start(out=selE_f[0:H, :], in_=selE_d), "DC", 16)
            E_["selE_ld"] = i

        @block.vector
        def _(q):
            q.memset(ones128[:, :], 1.0)
            q.memset(ones1[:, :], 1.0)
            q.memset(ones1f[:, :], 1.0)
            q.memset(onescol[:, :], 1.0)
            q.memset(cst[:, 0:1], EPS_LN)
            q.memset(cst[:, 1:2], -0.5)
            q.memset(cst[:, 2:3], 0.5)
            q.memset(cst[:, 3:4], 4.0 / DF)
            q.memset(cst[:, 4:5], 0.0)
            vv = vext.ap().rearrange("p j (c k) -> p j c k", k=129)
            q.memset(vv[:, :, :, 128:129], 1.0)
            q.memset(Zm[:, :, :], 0.0)
            wt = WT["vector"]
            wt(q, E_["selE_ld"])
            i = q.tensor_copy(out=Ec[0:H, :, :].rearrange("h c p -> h (c p)"),
                              in_=selE_f[0:H, :])
            E_["consts"] = inc(i, "V")

        wspec = [("wk", wkTe, wk_sb, wkrow), ("wv", wvTe, wv_sb, wvrow),
                 ("wq", wqT, wq_sb, None)]
        for name, wsrc, dst, rowdst in wspec:
            @block.scalar
            def _(q, name=name, wsrc=wsrc):
                wt = WT["scalar"]
                if name == "wv":
                    wt(q, E_["cast_wv_row"])
                if name == "wq":
                    wt(q, E_["cast_wq_pre"])
                i = inc(q.dma_start(out=stag[:, :, :],
                                    in_=wsrc[0:E, :].rearrange("(c p) f -> p c f", p=P)),
                        "AW", 16)
                E_["wload_" + name] = i

            @block.vector
            def _(q, name=name, dst=dst):
                wt = WT["vector"]
                wt(q, E_["wload_" + name])
                i = inc(q.tensor_copy(out=dst[:, :, :], in_=stag[:, :, :]), "V")
                E_["castm_" + name] = i

            if rowdst is not None:
                @block.scalar
                def _(q, name=name, wsrc=wsrc):
                    wt = WT["scalar"]
                    wt(q, E_["castm_" + name])
                    i = inc(q.dma_start(out=stag[0:1, 0, 0:E], in_=wsrc[E:E + 1, :]),
                            "AW", 16)
                    E_["wrowload_" + name] = i

                @block.vector
                def _(q, name=name, rowdst=rowdst):
                    wt = WT["vector"]
                    wt(q, E_["wrowload_" + name])
                    i = inc(q.tensor_copy(out=rowdst[0:1, :], in_=stag[0:1, 0, 0:E]), "V")
                    if name == "wk":
                        E_["cast_wv_row"] = i
                        E_["cast_wk"] = i
                    else:
                        E_["cast_wq_pre"] = i
                        E_["cast_wv"] = i
            else:
                E_["cast_wq"] = E_["castm_wq"]

        tern_idx = [0]

        def emit_tern_chunk(kind, r):
            gi = tern_idx[0]
            tern_idx[0] += 1
            sl = gi % 3

            @block.sync
            def _(q, kind=kind, r=r, gi=gi, sl=sl):
                wt = WT["sync"]
                if gi in (0, 1, 2):
                    wt(q, E_["alup_red", 15])
                if gi >= 3:
                    wt(q, E_["t_sg", gi - 3])
                if kind == "tu":
                    rc, hh = r // 4, r % 4
                    i = inc(q.dma_start(out=ternbh[:, sl, :],
                                        in_=wupT[rc * P:(rc + 1) * P, hh * E:(hh + 1) * E]),
                            f"GT{sl}", 16)
                else:
                    i = inc(q.dma_start(out=ternbh[:, sl, :],
                                        in_=wdnT[r * P:(r + 1) * P, 0:E]), f"GT{sl}", 16)
                E_["t_ld", gi] = i

            @block.scalar
            def _(q, kind=kind, gi=gi, sl=sl):
                wt = WT["scalar"]
                thr_t = thrU if kind == "tu" else thrD
                wt(q, E_["t_ld", gi], E_["thr_done", 0 if kind == "tu" else 1])
                if gi >= 3:
                    wt(q, E_["t_out", gi - 3])
                q.activation(out=terno[:, sl, :], in_=ternbh[:, sl, :],
                             func=AF.Sign, bias=thr_t[:, 1:2])
                i = inc(q.activation(out=tb2[:, sl, :], in_=ternbh[:, sl, :],
                                     func=AF.Sign, bias=thr_t[:, 0:1]), "S")
                E_["t_sg", gi] = i

            @block.gpsimd
            def _(q, kind=kind, r=r, gi=gi, sl=sl):
                wt = WT["gpsimd"]
                wt(q, E_["t_sg", gi])
                q.tensor_add(out=terno[:, sl, :], in0=terno[:, sl, :],
                             in1=tb2[:, sl, :])
                if kind == "tu":
                    rc, hh = r // 4, r % 4
                    i = inc(q.dma_start(out=wupt_d[rc * P:(rc + 1) * P, hh * E:(hh + 1) * E],
                                        in_=terno[:, sl, :]), f"GO{sl}", 16)
                else:
                    for dc in range(CE):
                        i = inc(q.dma_start(out=wdnt_d[dc, :, r, :],
                                            in_=terno[:, sl, dc * P:(dc + 1) * P]), f"GO{sl}", 16)
                E_["t_out", gi] = i
                if kind == "tu" and r == CF - 1:
                    E_["tu_done"] = i
                if kind == "td" and r == CF - 1:
                    E_["td_done"] = i

        TERN_SCHED = {4: [("tu", r) for r in range(0, 16)],
                      5: [("tu", r) for r in range(16, 32)],
                      6: [("td", r) for r in range(0, 16)],
                      7: [("td", r) for r in range(16, 32)]}

        def emit_alup(r):
            sl = r % 2

            @block.gpsimd
            def _(q, r=r, sl=sl):
                wt = WT["gpsimd"]
                if r < 2:
                    wt(q, E_["cast_wq"])
                else:
                    wt(q, E_["alup_red", r - 2])
                rc, hh = r // 2, r % 2
                i = inc(q.dma_start(out=ternb[:, sl * 2048:(sl + 1) * 2048],
                                    in_=wupT[rc * P:(rc + 1) * P, hh * 2048:(hh + 1) * 2048]),
                        f"GT{sl}", 16)
                E_["alup_ld", r] = i

            @block.vector
            def _(q, r=r, sl=sl):
                wt = WT["vector"]
                wt(q, E_["alup_ld", r])
                i = q.tensor_reduce(out=alpp[:, r:r + 1],
                                    in_=ternb[:, sl * 2048:(sl + 1) * 2048], axis=AX.X,
                                    op=ALU.add, apply_absolute_value=True)
                E_["alup_red", r] = inc(i, "V")

        def emit_aldn(r):
            @block.gpsimd
            def _(q, r=r):
                wt = WT["gpsimd"]
                if r >= 2:
                    wt(q, E_["aldn_red", r - 2])
                i = inc(q.dma_start(out=adbuf[:, r % 2, :],
                                    in_=wdnT[r * P:(r + 1) * P, 0:E]), f"GB{r % 2}", 16)
                E_["aldn_ld", r] = i

            @block.vector
            def _(q, r=r):
                wt = WT["vector"]
                wt(q, E_["aldn_ld", r])
                i = q.tensor_reduce(out=alpd[:, r:r + 1], in_=adbuf[:, r % 2, :],
                                    axis=AX.X, op=ALU.add,
                                    apply_absolute_value=True)
                E_["aldn_red", r] = inc(i, "V")

        BG_SCHED = {0: [(lambda r=r: emit_alup(r)) for r in range(8)],
                    1: [(lambda r=r: emit_alup(r)) for r in range(8, 16)],
                    2: [(lambda r=r: emit_aldn(r)) for r in range(16)],
                    3: [(lambda r=r: emit_aldn(r)) for r in range(16, 32)]}

        def emit_tern_some(t, n):
            lst = BG_SCHED.get(t)
            if lst:
                k = 1 if t < 2 else 2
                for _ in range(min(k if n == 2 else len(lst), len(lst))):
                    lst.pop(0)()
            lst = TERN_SCHED.get(t)
            if not lst:
                return
            for _ in range(min(n, len(lst))):
                kind, r = lst.pop(0)
                emit_tern_chunk(kind, r)

        p1_slot_mm = {0: None, 1: None}

        def emit_ln1(tt):
            buf = tt % 2
            for c in range(CE):
                prev_slot = p1_slot_mm[c % 2]

                @block.vector
                def _(q, tt=tt, c=c, buf=buf, prev_slot=prev_slot):
                    wt = WT["vector"]
                    wt(q, E_["x", tt])
                    if prev_slot is not None:
                        wt(q, prev_slot)
                    q.tensor_copy(out=xbfr[:, c % 2, :], in_=xa[:, buf, c, :])
                    i = inc(q.tensor_mul(out=xsqr[:, c % 2, :], in0=xbfr[:, c % 2, :],
                                         in1=xbfr[:, c % 2, :]), "V")
                    E_["p1sx", tt, c] = i

                @block.tensor
                def _(q, tt=tt, c=c):
                    wt = WT["tensor"]
                    if c == 0:
                        wt(q, E_["consts"])
                        if tt > 0:
                            wt(q, E_["p1facr", tt - 1], E_["qfree", tt - 1],
                               E_["qexpS", tt - 1])
                            if (tt - 1) in (2, 4):
                                wt(q, E_["thr_psfree", tt - 1])
                    wt(q, E_["p1sx", tt, c])
                    q.matmul(out=PS[0][:, :], lhsT=ones128[:, :], rhs=xbfr[:, c % 2, :],
                             start=(c == 0), stop=(c == CE - 1))
                    i = q.matmul(out=PS[1][:, :], lhsT=ones128[:, :], rhs=xsqr[:, c % 2, :],
                                 start=(c == 0), stop=(c == CE - 1))
                    i = inc(i, "T")
                    E_["p1smm", tt, c] = i
                    if c == CE - 1:
                        E_["p1sums", tt] = i

                p1_slot_mm[c % 2] = E_["p1smm", tt, c]

            @block.vector
            def _(q, tt=tt):
                wt = WT["vector"]
                wt(q, E_["p1sums", tt])
                q.tensor_scalar(out=st[0][:, :], in0=PS[0][:, :], scalar1=1.0 / E,
                                scalar2=None, op0=ALU.mult)
                q.tensor_mul(out=st[1][:, :], in0=st[0][:, :], in1=st[0][:, :])
                i = q.scalar_tensor_tensor(out=st[2][:, :], in0=PS[1][:, :], scalar=1.0 / E,
                                           in1=st[1][:, :], op0=ALU.mult, op1=ALU.subtract)
                E_["p1facr", tt] = inc(i, "V")

            @block.scalar
            def _(q, tt=tt):
                wt = WT["scalar"]
                wt(q, E_["p1facr", tt])
                q.activation(out=st[1][:, :], in_=st[2][:, :], func=AF.Ln, bias=cst[:, 0:1])
                i = inc(q.activation(out=st[3][:, :], in_=st[1][:, :], func=AF.Exp,
                                     scale=cst[:, 1:2]), "S")
                E_["p1s", tt] = i

            @block.vector
            def _(q, tt=tt, buf=buf):
                wt = WT["vector"]
                wt(q, E_["p1s", tt])
                if tt > 0:
                    wt(q, E_["xn_used", tt - 1])
                for c in range(CE):
                    q.tensor_sub(out=st[1][:, :], in0=xa[:, buf, c, :], in1=st[0][:, :])
                    i = inc(q.tensor_mul(out=xn[:, c, :], in0=st[1][:, :], in1=st[3][:, :]),
                            "V")
                    E_["xnc", tt, c] = i
                E_["xn", tt] = i
                E_["xa_free", tt] = i
        for t in range(NT):
            buf = t % 2
            t0 = t * TT

            if t + 1 < NT:
                @block.sync
                def _(q, t=t, t0=t0):
                    wt = WT["sync"]
                    if t - 1 >= 0:
                        wt(q, E_["xa_free", t - 1])
                    for c in range(CE):
                        i = inc(q.dma_start(out=xa[:, (t + 1) % 2, c, :],
                                            in_=xT[c * P:(c + 1) * P, t0 + TT:t0 + 2 * TT]),
                                f"DX{(t + 1) % 2}", 16)
                    E_["x", t + 1] = i

            if t == 0:
                emit_ln1(0)
            pend_smm = [None]

            for j in range(NSUB):
                @block.tensor
                def _(q, t=t, j=j):
                    wt = WT["tensor"]
                    if t == 0 and j == 0:
                        wt(q, E_["cast_wk"], E_["cast_wv"])
                    pj = (t, j - 1) if j > 0 else ((t - 1, NSUB - 1) if t > 0 else None)
                    if pj is not None:
                        wt(q, E_["kminV", pj[0], pj[1]], E_["kreluS", pj[0], pj[1]], E_["vfree", pj[0], pj[1]])
                    for c in range(CE):
                        wt(q, E_["xnc", t, c])
                        lt = xn[:, c, j * P:(j + 1) * P]
                        q.matmul(out=PS[1][:, :], lhsT=lt, rhs=wk_sb[:, c, 0:TT],
                                 start=(c == 0), stop=False)
                        q.matmul(out=PS[2][:, :], lhsT=lt, rhs=wk_sb[:, c, TT:E],
                                 start=(c == 0), stop=False)
                        q.matmul(out=PS[3][:, :], lhsT=lt, rhs=wv_sb[:, c, 0:TT],
                                 start=(c == 0), stop=False)
                        q.matmul(out=PS[4][:, :], lhsT=lt, rhs=wv_sb[:, c, TT:E],
                                 start=(c == 0), stop=False)
                    q.matmul(out=PS[1][:, :], lhsT=ones1[0:1, :], rhs=wkrow[0:1, 0:TT],
                             start=False, stop=True)
                    q.matmul(out=PS[2][:, :], lhsT=ones1[0:1, :], rhs=wkrow[0:1, TT:E],
                             start=False, stop=True)
                    q.matmul(out=PS[3][:, :], lhsT=ones1[0:1, :], rhs=wvrow[0:1, 0:TT],
                             start=False, stop=True)
                    i = q.matmul(out=PS[4][:, :], lhsT=ones1[0:1, :], rhs=wvrow[0:1, TT:E],
                                 start=False, stop=True)
                    E_["kv", t, j] = inc(i, "T")

                smm_prev = E_.get(("smm", t - 1, j))
                prev_paS = [pa_lastS[0], pa_lastS[1]]
                prev_pcV = [pc_lastV[0], pc_lastV[1]]

                @block.vector
                def _(q, t=t, j=j, smm_prev=smm_prev, prev_paS=prev_paS):
                    wt = WT["vector"]
                    wt(q, E_["kv", t, j])
                    if smm_prev is not None:
                        wt(q, smm_prev)
                    vv = vext.ap().rearrange("p j (c k) -> p j c k", k=129)
                    q.tensor_copy(out=vv[:, j, 0:4, 0:128],
                                  in_=PS[3].ap().rearrange("p (c k) -> p c k", k=128))
                    i = inc(q.tensor_copy(out=vv[:, j, 4:8, 0:128],
                                          in_=PS[4].ap().rearrange("p (c k) -> p c k", k=128)),
                            "V")
                    E_["vev", t, j] = i
                    E_["vfree", t, j] = i
                    if prev_paS[0] is not None:
                        wt(q, prev_paS[0], prev_paS[1])
                    q.tensor_scalar(out=pa[:, 0, :], in0=PS[1][:, :], scalar1=0.0,
                                    scalar2=None, op0=ALU.min)
                    i = inc(q.tensor_scalar(out=pa[:, 1, :], in0=PS[2][:, :], scalar1=0.0,
                                            scalar2=None, op0=ALU.min), "V")
                    E_["kminV", t, j] = i

                @block.scalar
                def _(q, t=t, j=j, prev_pcV=prev_pcV):
                    wt = WT["scalar"]
                    wt(q, E_["kminV", t, j])
                    q.activation(out=pa[:, 0, :], in_=pa[:, 0, :], func=AF.Exp)
                    q.activation(out=pa[:, 1, :], in_=pa[:, 1, :], func=AF.Exp)
                    if prev_pcV[0] is not None:
                        wt(q, prev_pcV[0], prev_pcV[1])
                    q.activation(out=pc[:, 0, :], in_=PS[1][:, :], func=AF.Relu)
                    i = inc(q.activation(out=pc[:, 1, :], in_=PS[2][:, :], func=AF.Relu), "S")
                    E_["kreluS", t, j] = i

                @block.vector
                def _(q, t=t, j=j):
                    wt = WT["vector"]
                    wt(q, E_["kreluS", t, j])
                    q.tensor_add(out=phiK[:, j, 0:TT], in0=pa[:, 0, :], in1=pc[:, 0, :])
                    i = inc(q.tensor_add(out=phiK[:, j, TT:E], in0=pa[:, 1, :],
                                         in1=pc[:, 1, :]), "V")
                    E_["phik", t, j] = i

                pa_lastS[0] = E_["kreluS", t, j]
                pa_lastS[1] = E_["kreluS", t, j]
                pc_lastV[0] = E_["phik", t, j]
                pc_lastV[1] = E_["phik", t, j]

                def emit_smm(t=t, j=j):
                    @block.tensor
                    def _(q, t=t, j=j):
                        wt = WT["tensor"]
                        wt(q, E_["phik", t, j], E_["vev", t, j])
                        sp_ = (t == NT - 1 and j == NSUB - 1)
                        for c in range(CE):
                            st_ = (t == 0 and j == 0 and c % 3 == 0)
                            i = q.matmul(out=s_slice(c),
                                         lhsT=phiK[:, j, c * P:(c + 1) * P],
                                         rhs=vext[:, j, c * 129:(c + 1) * 129],
                                         start=st_, stop=sp_)
                        E_["smm", t, j] = inc(i, "T")

                if pend_smm[0] is not None:
                    pend_smm[0]()
                pend_smm[0] = emit_smm

                emit_tern_some(t, 2)

            if pend_smm[0] is not None:
                pend_smm[0]()
                pend_smm[0] = None

            for c in range(CE):
                pq = PS[0] if c % 2 == 0 else PS[4]

                @block.tensor
                def _(q, t=t, c=c, pq=pq):
                    wt = WT["tensor"]
                    if t == 0 and c == 0:
                        wt(q, E_["cast_wq"])
                    if c >= 2:
                        wt(q, E_["qfreeV", t, c - 2], E_["qfreeS", t, c - 2])
                    elif c == 1:
                        wt(q, E_["vfree", t, NSUB - 1])
                    for e in range(CE):
                        i = q.matmul(out=pq[:, :], lhsT=wq_sb[:, e, c * P:(c + 1) * P],
                                     rhs=xn[:, e, :], start=(e == 0), stop=(e == CE - 1))
                    E_["qmm", t, c] = inc(i, "T")
                    if c == CE - 1:
                        E_["xn_used", t] = E_["qmm", t, c]

                extra = [E_["bias"]] if (t == 0 and c == 0) else None
                gprev = E_.get(("phiq_out", t - 1))
                extra_out = [gprev] if (c == 0 and gprev is not None) else None
                mq, rq, tq = emit_phi(c % 2, pq[:, :], bq_sb[:, c:c + 1],
                                      phiq_sb[:, c, :], E_["qmm", t, c],
                                      extra_wV=extra, extra_out_wV=extra_out)
                E_["qfreeV", t, c] = mq
                E_["qfreeS", t, c] = rq
                E_["phiq_tt", t, c] = tq
                if c % 2 == 0:
                    emit_tern_some(t, 2)

            E_["qfree", t] = E_["phiq_tt", t, CE - 1]
            E_["qexpS", t] = E_["qfreeS", t, CE - 1]

            @block.gpsimd
            def _(q, t=t, t0=t0):
                wt = WT["gpsimd"]
                wt(q, E_["phiq_tt", t, CE - 1])
                i = inc(q.dma_start(
                    out=phiq_d[:, t0:t0 + TT].rearrange("(c p) tt -> p c tt", p=P),
                    in_=phiq_sb[:, :, :]), "GQ", 16)
                E_["phiq_out", t] = i



            if t in (2, 4):
                which = 0 if t == 2 else 1
                col = slice(0, 1) if which == 0 else slice(1, 2)
                thr_t = thrU if which == 0 else thrD
                red_ev = E_["alup_red", 15] if which == 0 else E_["aldn_red", 31]
                parts_t = alpp if which == 0 else alpd

                nw = {0: 16, 1: CF}[which]

                @block.vector
                def _(q, t=t, red_ev=red_ev, parts_t=parts_t, nw=nw):
                    wt = WT["vector"]
                    wt(q, red_ev)
                    w2 = nw // 2
                    i = inc(q.tensor_add(out=alptree[:, 0:w2], in0=parts_t[:, 0:w2],
                                         in1=parts_t[:, w2:nw]), "V")
                    while w2 > 1:
                        wt(q, i)
                        w2 //= 2
                        i = inc(q.tensor_add(out=alptree[:, 0:w2], in0=alptree[:, 0:w2],
                                             in1=alptree[:, w2:2 * w2]), "V")
                    E_["altot", t] = i

                @block.gpsimd
                def _(q, t=t):
                    wt = WT["gpsimd"]
                    wt(q, E_["altot", t])
                    i = inc(q.dma_start(out=row128[0:1, 0:P], in_=alptree[:, 0:1],
                                        allow_slow_non_contiguous=True), "GF", 16)
                    E_["alrow", t] = i

                @block.vector
                def _(q, t=t):
                    wt = WT["vector"]
                    wt(q, E_["alrow", t])
                    w2 = P // 2
                    i = inc(q.tensor_add(out=row128[0:1, 0:w2], in0=row128[0:1, 0:w2],
                                         in1=row128[0:1, w2:P]), "V")
                    while w2 > 1:
                        wt(q, i)
                        w2 //= 2
                        i = inc(q.tensor_add(out=row128[0:1, 0:w2], in0=row128[0:1, 0:w2],
                                             in1=row128[0:1, w2:2 * w2]), "V")
                    wt(q, i)
                    q.tensor_scalar(out=throw[0:1, 0:1], in0=row128[0:1, 0:1],
                                    scalar1=0.5 / (DF * E), scalar2=None, op0=ALU.mult)
                    i = inc(q.tensor_scalar(out=throw[0:1, 1:2], in0=row128[0:1, 0:1],
                                            scalar1=-0.5 / (DF * E), scalar2=None,
                                            op0=ALU.mult), "V")
                    E_["throw", t] = i

                @block.tensor
                def _(q, t=t):
                    wt = WT["tensor"]
                    wt(q, E_["throw", t], E_["qfreeV", t, CE - 2], E_["qfreeS", t, CE - 2])
                    i = q.matmul(out=PS[0][:, 0:2], lhsT=ones1f[0:1, :], rhs=throw[0:1, 0:2],
                                 start=True, stop=True)
                    E_["thrbc_mm", t] = inc(i, "T")

                @block.vector
                def _(q, t=t, thr_t=thr_t, which=which):
                    wt = WT["vector"]
                    wt(q, E_["thrbc_mm", t])
                    i = inc(q.tensor_copy(out=thr_t[:, :], in_=PS[0][:, 0:2]), "V")
                    E_["thr_done", which] = i
                    E_["thr_psfree", t] = i

            emit_tern_some(t, 16)

            if t + 1 < NT:
                emit_ln1(t + 1)

        @block.vector
        def _(q):
            wt = WT["vector"]
            wt(q, E_["smm", NT - 1, NSUB - 1])
            for c in range(CE):
                i = q.tensor_copy(out=S_sb[:, c * 129:(c + 1) * 129], in_=s_slice(c))
            for c in range(CE):
                q.memset(S_sb[0:64, c * 129 + 64:c * 129 + 128], 0.0)
                i = q.memset(S_sb[64:128, c * 129:c * 129 + 64], 0.0)
            E_["sev"] = inc(i, "V")

        @block.gpsimd
        def _(q):
            wt = WT["gpsimd"]
            wt(q, E_["sev"])
            i = inc(q.dma_start(out=cc_in, in_=S_sb[:, :]), "GS", 16)
            wt(q, i)
            i = inc(q.collective_compute(
                "AllReduce", ALU.add,
                replica_groups=[[0, 1, 2, 3], [4, 5, 6, 7]],
                ins=[cc_in], outs=[cc_out]), "GS")
            wt(q, i)
            i = inc(q.dma_start(out=S_sb[:, :], in_=cc_out), "GS", 16)
            E_["sx"] = i

        @block.vector
        def _(q):
            wt = WT["vector"]
            wt(q, E_["sx"])
            q.tensor_copy(out=Sbf[:, :], in_=S_sb[:, :])
            for c in range(CE):
                q.tensor_copy(out=Zm[0:64, c, 2 * c:2 * c + 1],
                              in_=S_sb[0:64, c * 129 + 128:c * 129 + 129])
                i = q.tensor_copy(out=Zm[64:128, c, 2 * c + 1:2 * c + 2],
                                  in_=S_sb[64:128, c * 129 + 128:c * 129 + 129])
            E_["szm"] = inc(i, "V")

        for hh in range(2):
            @block.scalar
            def _(q, hh=hh):
                wt = WT["scalar"]
                wt(q, E_["qmm", NT - 1, CE - 1])
                if hh == 1:
                    wt(q, E_["wo_cast", 0])
                i = inc(q.dma_start(
                    out=wostag[:, :, :],
                    in_=woT[hh * 4 * P:(hh + 1) * 4 * P, :].rearrange("(c p) f -> p c f", p=P)),
                    "AW", 16)
                E_["wo_ld", hh] = i

            @block.vector
            def _(q, hh=hh):
                wt = WT["vector"]
                wt(q, E_["wo_ld", hh])
                i = inc(q.tensor_copy(
                    out=wo_sb[:, hh * 4:(hh + 1) * 4, :].rearrange("p c e -> p (c e)"),
                    in_=wostag[:, :, :].rearrange("p c e -> p (c e)")), "V")
                E_["wo_cast", hh] = i
        E_["wo"] = E_["wo_cast", 1]

        def ctx_ap(c):
            return phiK[:, c // 2, (c % 2) * TT:(c % 2 + 1) * TT]

        for t in range(NT):
            buf = t % 2
            t0 = t * TT

            def emit_den(tt):
                tt0 = tt * TT

                @block.gpsimd
                def _(q, tt=tt, tt0=tt0):
                    wt = WT["gpsimd"]
                    if tt == 0:
                        wt(q, E_["phiq_out", NT - 1])
                    else:
                        wt(q, E_["nd_used", tt - 1])
                    i = inc(q.dma_start(
                        out=phiq_sb[:, :, :],
                        in_=phiq_d[:, tt0:tt0 + TT].rearrange("(c p) tt -> p c tt", p=P)),
                        "GQ", 16)
                    E_["phiq_in", tt] = i

                @block.tensor
                def _(q, tt=tt):
                    wt = WT["tensor"]
                    wt(q, E_["phiq_in", tt], E_["szm"])
                    if tt > 0:
                        wt(q, E_["recv", tt - 1], E_["gelu", tt - 1, 29])
                    for c in range(CE):
                        i = q.matmul(out=PS[1][0:H, :], lhsT=Zm[:, c, :],
                                     rhs=phiq_sb[:, c, :],
                                     start=(c == 0), stop=(c == CE - 1))
                    E_["denmm", tt] = inc(i, "T")

                @block.vector
                def _(q, tt=tt):
                    wt = WT["vector"]
                    wt(q, E_["denmm", tt])
                    q.tensor_scalar(out=den16[0:H, :], in0=PS[1][0:H, :], scalar1=EPS_ATT,
                                    scalar2=None, op0=ALU.add)
                    q.reciprocal(out=den16[0:H, :], in_=den16[0:H, :])
                    i = inc(q.tensor_copy(out=rec16b[0:H, :], in_=den16[0:H, :]), "V")
                    E_["recv", tt] = i

            if t == 0:
                emit_den(0)

            @block.sync
            def _(q, t=t, t0=t0, buf=buf):
                wt = WT["sync"]
                if t >= 2:
                    wt(q, E_["r_done", t - 2])
                else:
                    wt(q, E_["xa_free", NT - 2 + t])
                for c in range(CE):
                    i = inc(q.dma_start(out=xa[:, buf, c, :],
                                        in_=xT[c * P:(c + 1) * P, t0:t0 + TT]), f"DX{buf}", 16)
                E_["x2", t] = i

            def emit_upld(g, t=t):
                @block.scalar
                def _(q, g=g, t=t):
                    wt = WT["scalar"]
                    if (t, g) == (0, 0):
                        wt(q, E_["tu_done"], E_["td_done"])
                    prev = (t, g - 2) if g >= 2 else ((t - 1, g + 6) if t > 0 else None)
                    if prev is not None:
                        wt(q, E_["upmm", prev[0], prev[1]])
                    i = inc(q.dma_start(
                        out=upbuf[:, g % 2, :, :],
                        in_=wupt_d.rearrange("(c p) f -> p c f", p=P)[:, :, g * TT:(g + 1) * TT]),
                        f"AU{g % 2}", 16)
                    E_["upld", t, g] = i

            def emit_dnld(dc, t=t):
                @block.scalar
                def _(q, dc=dc, t=t):
                    wt = WT["scalar"]
                    if (t, dc) == (0, 0):
                        wt(q, E_["td_done"])
                    if t == 0 and dc < 2:
                        wt(q, E_["wo"])
                    prev = (t, dc - 2) if dc >= 2 else ((t - 1, dc + 6) if t > 0 else None)
                    if prev is not None:
                        wt(q, E_["dnmm", prev[0], prev[1]])
                    i = inc(q.dma_start(out=dnbuf[:, dc % 2, 0:16, :],
                                        in_=wdnt_d[dc, :, 0:16, :]), f"AE{dc % 2}", 16)
                    E_["dnldA", t, dc] = i
                    i = inc(q.dma_start(out=dnbuf[:, dc % 2, 16:32, :],
                                        in_=wdnt_d[dc, :, 16:32, :]), f"AD{dc % 2}", 16)
                    E_["dnld", t, dc] = i

            emit_upld(0)
            emit_upld(1)
            emit_dnld(0)
            emit_dnld(1)

            def emit_den(tt):
                tt0 = tt * TT

                @block.gpsimd
                def _(q, tt=tt, tt0=tt0):
                    wt = WT["gpsimd"]
                    if tt == 0:
                        wt(q, E_["phiq_out", NT - 1])
                    else:
                        wt(q, E_["nd_used", tt - 1])
                    i = inc(q.dma_start(
                        out=phiq_sb[:, :, :],
                        in_=phiq_d[:, tt0:tt0 + TT].rearrange("(c p) tt -> p c tt", p=P)),
                        "GQ", 16)
                    E_["phiq_in", tt] = i

                @block.tensor
                def _(q, tt=tt):
                    wt = WT["tensor"]
                    wt(q, E_["phiq_in", tt], E_["szm"])
                    if tt > 0:
                        wt(q, E_["recv", tt - 1], E_["gelu", tt - 1, 29])
                    for c in range(CE):
                        i = q.matmul(out=PS[1][0:H, :], lhsT=Zm[:, c, :],
                                     rhs=phiq_sb[:, c, :],
                                     start=(c == 0), stop=(c == CE - 1))
                    E_["denmm", tt] = inc(i, "T")

                @block.vector
                def _(q, tt=tt):
                    wt = WT["vector"]
                    wt(q, E_["denmm", tt])
                    q.tensor_scalar(out=den16[0:H, :], in0=PS[1][0:H, :], scalar1=EPS_ATT,
                                    scalar2=None, op0=ALU.add)
                    q.reciprocal(out=den16[0:H, :], in_=den16[0:H, :])
                    i = inc(q.tensor_copy(out=rec16b[0:H, :], in_=den16[0:H, :]), "V")
                    E_["recv", tt] = i

            if t == 0:
                emit_den(0)
            for c in range(CE):
                pn = PS[0] if c % 2 == 0 else PS[4]
                pe_ = PS[2] if c % 2 == 0 else PS[6]

                @block.tensor
                def _(q, t=t, c=c, pn=pn, pe_=pe_):
                    wt = WT["tensor"]
                    wt(q, E_["phiq_in", t], E_["szm"])
                    if c >= 2:
                        wt(q, E_["ctxv", t, c - 2])
                    elif t > 0:
                        wt(q, E_["gelu", t - 1, 30 + c])
                        if c == 0:
                            wt(q, E_["outv", t - 1, 5])
                    i = q.matmul(out=pn[:, :], lhsT=Sbf[:, c * 129:c * 129 + 128],
                                 rhs=phiq_sb[:, c, :], start=True, stop=True)
                    E_["nummm", t, c] = inc(i, "T")
                    wt(q, E_["recv", t])
                    if c >= 2:
                        wt(q, E_["dbs", t, c - 2])
                    elif t > 0:
                        if c == 0:
                            wt(q, E_["b2ev", t - 1])
                    i = q.matmul(out=pe_[:, :], lhsT=Ec[0:H, c, :], rhs=rec16b[0:H, :],
                                 start=True, stop=True)
                    E_["ebc", t, c] = inc(i, "T")
                    if c == CE - 1:
                        E_["nd_used", t] = ev("T")

                @block.vector
                def _(q, t=t, c=c, pe_=pe_):
                    wt = WT["vector"]
                    wt(q, E_["ebc", t, c])
                    i = inc(q.tensor_copy(out=dbs_ap(c), in_=pe_[:, :]), "V")
                    E_["dbs", t, c] = i

                @block.vector
                def _(q, t=t, c=c, pn=pn):
                    wt = WT["vector"]
                    wt(q, E_["nummm", t, c], E_["dbs", t, c])
                    i = inc(q.tensor_mul(out=ctx_ap(c), in0=pn[:, :],
                                         in1=dbs_ap(c)), "V")
                    E_["ctxv", t, c] = i

            for c in range(CE):
                pat = PS[3] if c % 2 == 0 else PS[5]

                @block.tensor
                def _(q, t=t, c=c, pat=pat):
                    wt = WT["tensor"]
                    wt(q, E_["ctxv", t, CE - 1], E_["wo"])
                    if c >= 2:
                        wt(q, E_["rstt", t, c - 2])
                    elif t > 0:
                        wt(q, E_["outv", t - 1, 6 + c])
                    for e in range(CE):
                        i = q.matmul(out=pat[:, :], lhsT=wo_sb[:, e, c * P:(c + 1) * P],
                                     rhs=ctx_ap(e), start=(e == 0), stop=(e == CE - 1))
                    E_["attn", t, c] = inc(i, "T")

                @block.vector
                def _(q, t=t, c=c, pat=pat, buf=buf):
                    wt = WT["vector"]
                    wt(q, E_["attn", t, c], E_["x2", t])
                    if t == 0 and c == 0:
                        wt(q, E_["qfreeS", NT - 1, CE - 1])
                    i = inc(q.scalar_tensor_tensor(out=r_ap(c), in0=pat[:, :],
                                                   scalar=bo_sb[:, c:c + 1],
                                                   in1=xa[:, buf, c, :],
                                                   op0=ALU.add, op1=ALU.add), "V")
                    E_["rstt", t, c] = i

            E_["r_done", t] = E_["rstt", t, CE - 1]

            for c in range(CE):
                prev_slot = p1_slot_mm[c % 2]

                @block.vector
                def _(q, t=t, c=c, prev_slot=prev_slot):
                    wt = WT["vector"]
                    wt(q, E_["rstt", t, c])
                    if prev_slot is not None:
                        wt(q, prev_slot)
                    q.tensor_copy(out=xbfr[:, c % 2, :], in_=r_ap(c))
                    i = inc(q.tensor_mul(out=xsqr[:, c % 2, :], in0=xbfr[:, c % 2, :],
                                         in1=xbfr[:, c % 2, :]), "V")
                    E_["p2sx", t, c] = i

                @block.tensor
                def _(q, t=t, c=c):
                    wt = WT["tensor"]
                    if c == 0:
                        wt(q, E_["dbs", t, CE - 1])
                        if t > 0:
                            wt(q, E_["p2facr", t - 1])
                    wt(q, E_["p2sx", t, c])
                    q.matmul(out=PS[6][:, :], lhsT=ones128[:, :], rhs=xbfr[:, c % 2, :],
                             start=(c == 0), stop=(c == CE - 1))
                    i = q.matmul(out=PS[7][:, :], lhsT=ones128[:, :], rhs=xsqr[:, c % 2, :],
                                 start=(c == 0), stop=(c == CE - 1))
                    i = inc(i, "T")
                    E_["p2smm", t, c] = i
                    if c == CE - 1:
                        E_["p2sums", t] = i

                p1_slot_mm[c % 2] = E_["p2smm", t, c]

            @block.vector
            def _(q, t=t):
                wt = WT["vector"]
                wt(q, E_["p2sums", t])
                q.tensor_scalar(out=st[0][:, :], in0=PS[6][:, :], scalar1=1.0 / E,
                                scalar2=None, op0=ALU.mult)
                q.tensor_mul(out=st[1][:, :], in0=st[0][:, :], in1=st[0][:, :])
                i = q.scalar_tensor_tensor(out=st[2][:, :], in0=PS[7][:, :], scalar=1.0 / E,
                                           in1=st[1][:, :], op0=ALU.mult, op1=ALU.subtract)
                E_["p2facr", t] = inc(i, "V")

            @block.scalar
            def _(q, t=t):
                wt = WT["scalar"]
                wt(q, E_["p2facr", t])
                q.activation(out=st[1][:, :], in_=st[2][:, :], func=AF.Sqrt, bias=cst[:, 4:5])
                i = inc(q.activation(out=st[2][:, :], in_=st[2][:, :], func=AF.Sqrt,
                                     bias=cst[:, 0:1]), "S")
                E_["p2sq", t] = i

            @block.vector
            def _(q, t=t):
                wt = WT["vector"]
                wt(q, E_["p2sq", t])
                q.scalar_tensor_tensor(out=st[3][:, :], in0=st[2][:, :], scalar=EPS_RMS,
                                       in1=st[1][:, :], op0=ALU.mult, op1=ALU.add)
                q.reciprocal(out=st[3][:, :], in_=st[3][:, :])
                if t > 0:
                    wt(q, E_["xn2_used", t - 1])
                for c in range(CE):
                    q.tensor_sub(out=st[1][:, :], in0=r_ap(c), in1=st[0][:, :])
                    i = inc(q.tensor_mul(out=xn[:, c, :], in0=st[1][:, :], in1=st[3][:, :]),
                            "V")
                    E_["xn2c", t, c] = i
                E_["xn2", t] = i

            UPPS = [PS[0], PS[4], PS[1]]

            def emit_h2(fc, t=t):
                @block.tensor
                def _(q, t=t, fc=fc):
                    wt = WT["tensor"]
                    wt(q, E_["hsq", t, fc])
                    if t > 0 and fc == 0:
                        wt(q, E_["b2ev", t - 1])
                    i = q.matmul(out=PS[2][:, :], lhsT=ones128[:, :],
                                 rhs=hsqr[:, fc % 4, :],
                                 start=(fc == 0), stop=(fc == CF - 1))
                    E_["h2mm", t, fc] = inc(i, "T")

            for g in range(CE):
                if g >= 2:
                    emit_upld(g)
                for fo in range(4):
                    fc = g * 4 + fo
                    pu = UPPS[fc % 3]

                    @block.tensor
                    def _(q, t=t, g=g, fc=fc, pu=pu):
                        wt = WT["tensor"]
                        wt(q, E_["upld", t, g])
                        if fc >= 3:
                            wt(q, E_["gelu", t, fc - 3])
                        elif fc == 2:
                            wt(q, E_["recv", t])
                            if t > 0:
                                wt(q, E_["gelu", t - 1, 29])
                        for e in range(CE):
                            wt(q, E_["xn2c", t, e])
                            i = q.matmul(out=pu[:, :],
                                         lhsT=upbuf[:, g % 2, e, (fc % 4) * P:(fc % 4 + 1) * P],
                                         rhs=xn[:, e, :], start=(e == 0), stop=(e == CE - 1))
                        E_["upmmc", t, fc] = inc(i, "T")
                        if fc % 4 == 3:
                            E_["upmm", t, g] = ev("T")
                        if fc == CF - 1:
                            E_["xn2_used", t] = ev("T")
                            E_["up_psfree", t] = ev("T")

                    if fc >= 2:
                        emit_h2(fc - 2)

                    @block.scalar
                    def _(q, t=t, fc=fc, pu=pu):
                        wt = WT["scalar"]
                        wt(q, E_["upmmc", t, fc])
                        i = inc(q.activation(out=h_sb[:, fc, :], in_=pu[:, :], func=AF.Gelu,
                                             scale=cst[:, 2:3]), "S")
                        E_["gelu", t, fc] = i

                    @block.vector
                    def _(q, t=t, fc=fc):
                        wt = WT["vector"]
                        wt(q, E_["gelu", t, fc])
                        prev = (t, fc - 4) if fc >= 4 else ((t - 1, fc + CF - 4) if t > 0 else None)
                        if prev is not None:
                            wt(q, E_["h2mm", prev[0], prev[1]])
                        i = inc(q.tensor_mul(out=hsqr[:, fc % 4, :], in0=h_sb[:, fc, :],
                                             in1=h_sb[:, fc, :]), "V")
                        E_["hsq", t, fc] = i

            emit_h2(CF - 2)
            emit_h2(CF - 1)

            if t + 1 < NT:
                emit_den(t + 1)

            @block.scalar
            def _(q, t=t):
                wt = WT["scalar"]
                wt(q, E_["h2mm", t, CF - 1])
                i = inc(q.activation(out=b2_sb[:, :], in_=PS[2][:, :], func=AF.Sqrt,
                                     scale=cst[:, 3:4]), "S")
                E_["b2ev", t] = i

            @block.vector
            def _(q, t=t):
                wt = WT["vector"]
                wt(q, E_["b2ev", t])
                q.tensor_scalar(out=b2_sb[:, :], in0=b2_sb[:, :], scalar1=2.0 * EPS_RMS,
                                scalar2=None, op0=ALU.add)
                i = inc(q.reciprocal(out=b2_sb[:, :], in_=b2_sb[:, :]), "V")
                E_["rb", t] = i

            for dc in range(CE):
                if dc >= 2:
                    emit_dnld(dc)
                pd = [PS[3], PS[5], PS[0]][dc % 3]

                @block.tensor
                def _(q, t=t, dc=dc, pd=pd):
                    wt = WT["tensor"]
                    wt(q, E_["gelu", t, CF - 1], E_["dnldA", t, dc])
                    if dc >= 3:
                        wt(q, E_["outv", t, dc - 3])
                    elif dc < 2:
                        wt(q, E_["rstt", t, 6 + dc])
                    for fc in range(CF):
                        if fc == 16:
                            wt(q, E_["dnld", t, dc])
                        i = q.matmul(out=pd[:, :], lhsT=dnbuf[:, dc % 2, fc, :],
                                     rhs=h_sb[:, fc, :], start=(fc == 0), stop=(fc == CF - 1))
                    E_["dnmm", t, dc] = inc(i, "T")

                @block.vector
                def _(q, t=t, dc=dc, pd=pd):
                    wt = WT["vector"]
                    wt(q, E_["dnmm", t, dc], E_["rb", t])
                    prev = (t, dc - 2) if dc >= 2 else ((t - 1, dc + 6) if t > 0 else None)
                    if prev is not None:
                        wt(q, E_["ost", prev[0], prev[1]])
                    q.tensor_mul(out=t4[:, :], in0=pd[:, :], in1=b2_sb[:, :])
                    i = inc(q.tensor_add(out=out_sb[:, dc % 2, :], in0=t4[:, :],
                                         in1=r_ap(dc)), "V")
                    E_["outv", t, dc] = i

                @block.sync
                def _(q, t=t, dc=dc, t0=t0):
                    wt = WT["sync"]
                    wt(q, E_["outv", t, dc])
                    i = inc(q.dma_start(out=outT[dc * P:(dc + 1) * P, t0:t0 + TT],
                                        in_=out_sb[:, dc % 2, :]), f"DO{dc % 2}", 16)
                    E_["ost", t, dc] = i

        @block.sync
        def _(q):
            q.wait_ge(sem["DO0"], cnt["DO0"])
            q.wait_ge(sem["DO1"], cnt["DO1"])

        import os
        if os.environ.get("KBUILD_SYNCCHECK"):
            semname = {id(h): n for n, h in sem.items()}

            class _RecQ:
                def __init__(self, items):
                    self.items = items

                def wait_ge(self, h, v):
                    self.items.append(("wait", semname[id(h)], v))

                def __getattr__(self, name):
                    items = self.items

                    class _RI:
                        def __init__(self):
                            items.append(("instr", name, []))
                            self.slot = items[-1]

                        def then_inc(self, h, n):
                            self.slot[2].append((semname[id(h)], n))
                            return self

                    def f(*a, **k):
                        return _RI()
                    return f

            streams = {}
            for s in cnt:
                cnt[s] = 0
            waited.clear()
            for ename in ENGINES:
                items = []
                rq = _RecQ(items)
                for fn in SB.lists[ename]:
                    fn(rq)
                streams[ename] = items
            return streams

        for s in cnt:
            cnt[s] = 0
        waited.clear()
        with nc.Block() as rb:
            for ename in ENGINES:
                fns = SB.lists[ename]

                def flush(q, fns=fns):
                    for fn in fns:
                        fn(q)

                getattr(rb, ename)(flush)

    return nc


_NC_CACHE = {}


def kernel(L0_attn_query, L0_attn_Wq, L0_attn_bq, L0_attn_Wk, L0_attn_bk,
           L0_attn_Wv, L0_attn_bv, L0_attn_Wo, L0_attn_bo,
           L0_ffn_w_up, L0_ffn_w_down):
    x = np.asarray(L0_attn_query)
    B, L, _ = x.shape
    wqT_h = np.ascontiguousarray(np.asarray(L0_attn_Wq).T)
    wkTe_h = np.ascontiguousarray(np.concatenate(
        [np.asarray(L0_attn_Wk).T, np.asarray(L0_attn_bk)[None, :]], axis=0))
    wvTe_h = np.ascontiguousarray(np.concatenate(
        [np.asarray(L0_attn_Wv).T, np.asarray(L0_attn_bv)[None, :]], axis=0))
    woT_h = np.ascontiguousarray(np.asarray(L0_attn_Wo).T)
    wupT_h = np.ascontiguousarray(np.asarray(L0_ffn_w_up).T)
    wdnT_h = np.ascontiguousarray(np.asarray(L0_ffn_w_down).T)
    bq_h = np.ascontiguousarray(np.asarray(L0_attn_bq))
    bo_h = np.ascontiguousarray(np.asarray(L0_attn_bo))

    selE = np.zeros((H, CE * P), dtype=np.float32)
    for c in range(CE):
        selE[2 * c, c * P:c * P + 64] = 1.0
        selE[2 * c + 1, c * P + 64:(c + 1) * P] = 1.0

    if "nc" not in _NC_CACHE:
        _NC_CACHE["nc"] = build_program()
    nc = _NC_CACHE["nc"]

    in_maps = []
    for c in range(NCORE):
        b, qr = c // 4, c % 4
        xs = np.ascontiguousarray(x[b, qr * TPC:(qr + 1) * TPC, :].T)
        in_maps.append({
            "xT": xs, "wqT": wqT_h, "wkTe": wkTe_h, "wvTe": wvTe_h, "woT": woT_h,
            "bq": bq_h, "bo": bo_h, "wupT": wupT_h, "wdnT": wdnT_h, "selE": selE,
        })

    res = run_bass_kernel_spmd(nc, in_maps, core_ids=list(range(NCORE)))
    out = np.empty((B, L, E), dtype=np.float32)
    for c in range(NCORE):
        b, qr = c // 4, c % 4
        out[b, qr * TPC:(qr + 1) * TPC, :] = res.results[c]["outT"].T
    return out
